# version 15
# baseline (speedup 1.0000x reference)
import math
from contextlib import ExitStack

import numpy as np

import concourse.bass as bass
import concourse.mybir as mybir
from concourse.bass_utils import run_bass_kernel_spmd

F32 = mybir.dt.float32
BF16 = mybir.dt.bfloat16
AX = mybir.AxisListType
ALU = mybir.AluOpType
ACT = mybir.ActivationFunctionType

ENGINES = ("sync", "gpsimd", "vector", "scalar", "tensor")

FULL = dict(C=2048, T=4096, TOWN=2048, H=16, G=4, DFF=5504, NB=2)


class Sem:
    __slots__ = ("h", "n", "name")

    def __init__(self, es, nc, name):
        self.h = es.enter_context(nc.semaphore(name))
        self.n = 0
        self.name = name


class Rec:

    def __init__(self):
        self.stream = {e: [] for e in ENGINES}

    def add(self, eng, fn, deps, sem, n):
        self.stream[eng].append((fn, tuple(deps), sem, n))


def build(cfg, dbg=False):
    C = cfg["C"]; T = cfg["T"]; TOWN = cfg["TOWN"]; H = cfg["H"]; G = cfg["G"]
    DFF = cfg["DFF"]; NB = cfg["NB"]
    CC = C // 128
    NT = T // 128
    NJ = TOWN // 128
    KV = G * 128
    HP = H // 2
    NF = DFF // 128
    BT = TOWN // NB
    NTT = BT // 128
    NSP = cfg.get("NSP", min(512, BT))
    NSB = BT // NSP
    NV = NT + 1
    GRP = H // G
    SCALE_IN = 1.0 / (50.0 * math.sqrt(128.0))
    EXPM50 = math.exp(-50.0)

    nc = bass.Bass("TRN2", target_bir_lowering=False, debug=False)

    def din(name, shape, dt=F32):
        return nc.dram_tensor(name, shape, dt, kind="ExternalInput").ap()

    xf = din("xf", [T, C])
    xo = din("xo", [TOWN, C])
    wq = din("wq", [G, C, 512])
    wk = din("wk", [C, KV])
    wv = din("wv", [C, KV])
    wo = din("wo", [C, C])
    qnw = din("qnw", [128])
    knw = din("knw", [128])
    bo = din("bo", [C])
    ebias = din("ebias", [8])
    wgu = din("wgu", [NF, 2, C, 128])
    wd = din("wd", [CC, DFF, 128])
    agu = din("agu", [C, 152])
    ad = din("ad", [DFF, 72])
    bg = din("bg", [72, DFF])
    bu = din("bu", [72, DFF])
    bd = din("bd", [72, C])
    cosg = din("cosg", [T, 64])
    sing = din("sing", [T, 64])
    coso = din("coso", [TOWN, 64])
    sino = din("sino", [TOWN, 64])
    cmaskin = din("cmaskin", [128, 256])
    cvals = din("cvals", [2])
    smask = din("smask", [T, NV], BF16)
    idf_in = din("idf_in", [128, 128])
    idb_in = din("idb_in", [128, 128], BF16)

    out = nc.dram_tensor("out", [TOWN, C], F32, kind="ExternalOutput").ap()
    yt = nc.dram_tensor("yt", [H * 128, TOWN], BF16, kind="Internal").ap()

    es = ExitStack()
    with es:
        pb = [es.enter_context(nc.psum_tensor(f"pb{i}", [128, 512], F32))
              for i in range(8)]
        pbb = [t[:].bitcast(BF16) for t in pb]
        sv = {e: Sem(es, nc, "se_" + e) for e in ENGINES}
        ph = Sem(es, nc, "ph")
        dsem = {}

        def DS(name):
            if name not in dsem:
                dsem[name] = Sem(es, nc, "d_" + name)
            return dsem[name]

        cvr = es.enter_context(nc.sbuf_tensor("cvr", [128, 2], F32))
        waited = {e: {} for e in ENGINES}
        cur = {"rec": None}
        n_phases = [0]

        def op(eng, fn, deps=(), inc=True, n=1):
            sem = sv[eng] if inc else None
            if sem is not None:
                sem.n += n
            cur["rec"].add(eng, fn, deps, sem, n)
            return (sem, sem.n) if sem else None

        def dma(q, dst, src, stream, deps=()):
            s = DS(stream)
            s.n += 16
            cur["rec"].add(q, (lambda e, d=dst, sc=src: e.dma_start(d, sc)),
                           deps, s, 16)
            return (s, s.n)

        def phase(name):
            cur["rec"] = Rec()

        def run_phase(dma_waits=()):
            rec = cur["rec"]
            k = n_phases[0]
            for e in ENGINES:
                deps = [(ph, 5 * k)] if k > 0 else []
                rec.stream[e].insert(0, (None, tuple(deps), None, 0))
            for e in ENGINES:
                deps = list(dma_waits) if e == "sync" else []
                ph.n += 1
                rec.stream[e].append(
                    (lambda eng: eng.nop(), tuple(deps), ph, 1))
            with nc.Block() as blk:
                for e in ENGINES:
                    ops = rec.stream[e]

                    def body(engine, ops=ops, e=e):
                        seen = waited[e]
                        for fn, deps, sem, n in ops:
                            for (s, v) in deps:
                                if seen.get(s.name, 0) < v:
                                    engine.wait_ge(s.h, v)
                                    seen[s.name] = v
                            if fn is None:
                                continue
                            ins = fn(engine)
                            if sem is not None:
                                ins.then_inc(sem.h, n)
                    getattr(blk, e)(body)
            n_phases[0] += 1
            cur["rec"] = None

        def sb(scope, name, shape, dt):
            return scope.enter_context(nc.sbuf_tensor(name, shape, dt))

        bank_deps = {i: [] for i in range(8)}

        def bank_dep(i):
            return bank_deps[i]

        def set_bank_dep(i, toks):
            bank_deps[i] = list(toks)

        NSS = C // 512

        sAB = ExitStack()
        with sAB:
            kf = sb(sAB, "kf", [128, G * T], BF16)
            vt = sb(sAB, "vt", [128, NT * KV], BF16)
            vsufT = sb(sAB, "vsufT", [128, G * NV], F32)
            onesm = sb(sAB, "onesm", [128, 128], BF16)
            qnwr = sb(sAB, "qnwr", [128, 128], F32)
            knwr = sb(sAB, "knwr", [128, 128], F32)
            cmsk = sb(sAB, "cmsk", [128, 256], F32)
            cosob = sb(sAB, "cosob", [128, NJ * 64], BF16)
            sinob = sb(sAB, "sinob", [128, NJ * 64], BF16)
            idf = sb(sAB, "idf", [128, 128], F32)
            idb = sb(sAB, "idb", [128, 128], BF16)

            def emit_ln(xtile, dep_load, lnb, stats, stg):
                lnb_free = cur.get("lnb_free", [])
                sqs = []
                for i in range(NSS):
                    sqs.append(op("scalar", lambda e, x=xtile, i=i, stg=stg,
                                  stats=stats: e.activation(
                        stg[:, i * 512:(i + 1) * 512],
                        x[:, i * 512:(i + 1) * 512],
                        ACT.Square, accum_out=stats[:, i:i + 1]),
                        deps=[dep_load] + (lnb_free if i == 0 else [])))
                tsum = op("vector", lambda e, x=xtile, stats=stats:
                          e.reduce_sum(stats[:, 4:5], x[:], axis=AX.X),
                          deps=[dep_load])
                tmu = op("vector", lambda e, stats=stats: e.tensor_scalar_mul(
                    stats[:, 5:6], stats[:, 4:5], -1.0 / C), deps=[tsum])
                tm2 = op("vector", lambda e, stats=stats: e.tensor_mul(
                    stats[:, 6:7], stats[:, 5:6], stats[:, 5:6]), deps=[tmu])
                tex = op("vector", lambda e, stats=stats: e.reduce_sum(
                    stats[:, 7:8], stats[:, 0:NSS], axis=AX.X), deps=sqs)
                tvar = op("vector", lambda e, stats=stats: e.tensor_scalar(
                    stats[:, 8:9], stats[:, 7:8], 1.0 / C, stats[:, 6:7],
                    op0=ALU.mult, op1=ALU.subtract), deps=[tex, tm2])
                tsd = op("scalar", lambda e, stats=stats: e.activation(
                    stats[:, 9:10], stats[:, 8:9], ACT.Sqrt,
                    bias=cvr[:, 0:1]), deps=[tvar])
                trs = op("vector", lambda e, stats=stats: e.reciprocal(
                    stats[:, 10:11], stats[:, 9:10]), deps=[tsd])
                tln = op("vector", lambda e, x=xtile, lnb=lnb, stats=stats:
                         e.tensor_scalar(
                    lnb[:], x[:], stats[:, 5:6], stats[:, 10:11],
                    op0=ALU.add, op1=ALU.mult),
                    deps=[tmu, trs] + sqs + lnb_free)
                return tln

            def emit_transposes(dst_fn, tln, bank0, bank1, lnb):
                toks = []
                last_tt = None
                for grp in range(CC // 4):
                    bi = bank0 if grp % 2 == 0 else bank1
                    tts = []
                    for q in range(4):
                        cc = grp * 4 + q
                        tts.append(op("tensor", lambda e, bi=bi, q=q, cc=cc,
                                      lnb=lnb: e.transpose(
                            pbb[bi][:, q * 128:(q + 1) * 128],
                            lnb[:, cc * 128:(cc + 1) * 128],
                            idb[:]),
                            deps=([tln] + bank_dep(bi)) if q == 0 else []))
                    tev = op("vector", lambda e, bi=bi, g=grp:
                             e.tensor_copy(dst_fn(g), pbb[bi][:, 0:512]),
                             deps=[tts[-1]] + tts[:-1])
                    set_bank_dep(bi, [tev])
                    toks.append(tev)
                    last_tt = tts[-1]
                cur["lnb_free"] = [last_tt]
                return toks

            sA1 = ExitStack()
            with sA1:
                cosgb = sb(sA1, "cosgb", [128, NT * 64], BF16)
                singb = sb(sA1, "singb", [128, NT * 64], BF16)
                wkb = sb(sA1, "wkb", [128, CC * KV], BF16)
                wvb = sb(sA1, "wvb", [128, CC * KV], BF16)
                smb = sb(sA1, "smb", [128, NT * NV], BF16)
                xb = [sb(sA1, f"xb{i}", [128, C], F32) for i in range(2)]
                lnb = sb(sA1, "lnb", [128, C], BF16)
                lnscr = sb(sA1, "lnscr", [128, C], BF16)
                knsb = sb(sA1, "knsb", [128, KV], F32)
                krb = sb(sA1, "krb", [128, 2 * KV], BF16)
                rt1 = sb(sA1, "rt1", [128, 256], F32)
                rt2 = sb(sA1, "rt2", [128, 256], F32)
                stats = sb(sA1, "stats", [128, 16], F32)
                vsuf = sb(sA1, "vsuf", [NV, KV], BF16)
                stg = sb(sA1, "stg", [128, max(C, 512, max(NT, NJ) * 64)], F32)

                phase("p0")
                dma("sync", cvr[:], cvals[None, :].broadcast_to([128, 2]), "res")
                dma("sync", idf[:], idf_in, "res")
                dma("sync", idb[:], idb_in, "res")
                dma("sync", qnwr[:], qnw[None, :].broadcast_to([128, 128]), "res")
                dma("sync", knwr[:], knw[None, :].broadcast_to([128, 128]), "res")
                dma("sync", cmsk[:], cmaskin, "res")
                dma("gpsimd", wkb[:].rearrange("p (cc m) -> p cc m", cc=CC),
                    wk.rearrange("(cc p) m -> p cc m", p=128), "wkv")
                dma("gpsimd", wvb[:].rearrange("p (cc m) -> p cc m", cc=CC),
                    wv.rearrange("(cc p) m -> p cc m", p=128), "wkv")
                dma("gpsimd", smb[:].rearrange("p (t v) -> p t v", t=NT),
                    smask.rearrange("(t p) v -> p t v", p=128), "wkv")
                for (src, dst, nblk) in ((cosg, cosgb, NT), (sing, singb, NT),
                                         (coso, cosob, NJ), (sino, sinob, NJ)):
                    t = dma("sync", stg[:, 0:nblk * 64].rearrange(
                        "p (t v) -> p t v", t=nblk),
                        src.rearrange("(t p) v -> p t v", p=128), "tab",
                        deps=[cur["last_cast"]] if "last_cast" in cur else [])
                    cur["last_cast"] = op(
                        "vector", lambda e, d=dst, s=stg, nb=nblk:
                        e.tensor_copy(d[:, 0:nb * 64], s[:, 0:nb * 64]),
                        deps=[t])
                op("vector", lambda e: e.memset(onesm[:], 1.0))
                op("vector", lambda e: e.memset(vsuf[:, :], 0.0))
                run_phase(dma_waits=[(DS("res"), DS("res").n),
                                     (DS("tab"), DS("tab").n)])

                phase("a1")
                kpend = [None]

                def emit_ktr(u):
                    g2, t3k, t6k, slk = u
                    ktt = []
                    for gg in range(G):
                        ktt.append(op("tensor", lambda e, gg=gg, slk=slk:
                                      e.transpose(
                            pbb[4][:, gg * 128:(gg + 1) * 128],
                            krb[:, slk + gg * 128:slk + (gg + 1) * 128],
                            idb[:]),
                            deps=([t3k, t6k] + bank_dep(4)) if gg == 0 else []))
                    cur[f"krb_free{slk}"] = [ktt[-1]] + ktt[:-1]
                    kdst = kf[:].rearrange("p (g t) -> p g t", g=G)
                    tke = op("vector", lambda e, g2=g2, kdst=kdst:
                             e.tensor_copy(
                        kdst[:, :, g2 * 128:(g2 + 1) * 128],
                        pbb[4][:, 0:G * 128].rearrange(
                            "p (g z) -> p g z", g=G)),
                        deps=[ktt[-1]] + ktt[:-1])
                    set_bank_dep(4, [tke])
                resdep = [(DS("res"), DS("res").n), (DS("wkv"), DS("wkv").n),
                          (sv["vector"], sv["vector"].n)]

                NSS = C // 512

                for g in range(NT):
                    buf = xb[g % 2]
                    ld = dma("sync", buf[:], xf[g * 128:(g + 1) * 128, :],
                             f"x{g % 2}", deps=cur.get(f"xfree{g % 2}", []))
                    tln = emit_ln(buf, ld, lnb, stats, stg)
                    cur[f"xfree{g % 2}"] = [tln]
                    toks = emit_transposes(
                        lambda gr: lnscr[:, gr * 512:(gr + 1) * 512],
                        tln, 0, 1, lnb)
                    kdeps = resdep + toks + bank_dep(2) + bank_dep(3)
                    mmk = mmv = None
                    for cc in range(CC):
                        mmk = op("tensor", lambda e, cc=cc, first=(cc == 0),
                                 last=(cc == CC - 1): e.matmul(
                            pb[2][:, 0:KV],
                            lnscr[:, cc * 128:(cc + 1) * 128],
                            wkb[:, cc * KV:(cc + 1) * KV],
                            start=first, stop=last),
                            deps=kdeps if cc == 0 else [], inc=(cc == CC - 1))
                        mmv = op("tensor", lambda e, cc=cc, first=(cc == 0),
                                 last=(cc == CC - 1): e.matmul(
                            pb[3][:, 0:KV],
                            lnscr[:, cc * 128:(cc + 1) * 128],
                            wvb[:, cc * KV:(cc + 1) * KV],
                            start=first, stop=last),
                            deps=[], inc=(cc == CC - 1))
                    tve = op("vector", lambda e, g=g: e.tensor_copy(
                        vt[:, g * KV:(g + 1) * KV], pb[3][:, 0:KV]), deps=[mmv])
                    set_bank_dep(3, [tve])
                    qs = []
                    for gg in range(G):
                        qs.append(op("scalar", lambda e, gg=gg: e.activation(
                            stg[:, gg * 128:(gg + 1) * 128],
                            pb[2][:, gg * 128:(gg + 1) * 128], ACT.Square,
                            accum_out=stats[:, 11 + gg:12 + gg]), deps=[mmk]))
                    trms = op("scalar", lambda e: e.activation(
                        stats[:, 11:11 + G], stats[:, 11:11 + G], ACT.Sqrt,
                        scale=1.0 / 128.0), deps=qs)
                    tfac = op("vector", lambda e: e.tensor_scalar_add(
                        stats[:, 11:11 + G], stats[:, 11:11 + G], 1e-6), deps=[trms])
                    tfac = op("vector", lambda e: e.reciprocal(
                        stats[:, 11:11 + G], stats[:, 11:11 + G]), deps=[tfac])
                    kapp = []
                    for gg in range(G):
                        kapp.append(op("vector", lambda e, gg=gg:
                                       e.scalar_tensor_tensor(
                            knsb[:, gg * 128:(gg + 1) * 128],
                            pb[2][:, gg * 128:(gg + 1) * 128],
                            stats[:, 11 + gg:12 + gg], knwr[:],
                            op0=ALU.mult, op1=ALU.mult), deps=[tfac, mmk]))
                    set_bank_dep(2, kapp)
                    slk = (g % 2) * KV
                    kn3 = knsb[:].rearrange("p (g z) -> p g z", g=G)
                    kr3 = krb[:, slk:slk + KV].rearrange(
                        "p (g z) -> p g z", g=G)
                    cb = cosgb[:, g * 64:(g + 1) * 64].unsqueeze(1) \
                        .broadcast_to([128, G, 64])
                    sbr = singb[:, g * 64:(g + 1) * 64].unsqueeze(1) \
                        .broadcast_to([128, G, 64])
                    r1 = rt1[:, 0:G * 64].rearrange("p (g z) -> p g z", g=G)
                    r2 = rt2[:, 0:G * 64].rearrange("p (g z) -> p g z", g=G)
                    t1 = op("vector", lambda e, kn3=kn3, cb=cb, r1=r1:
                            e.tensor_tensor(r1, kn3[:, :, 0:64], cb,
                                            op=ALU.mult), deps=kapp)
                    t2 = op("vector", lambda e, kn3=kn3, sbr=sbr, r2=r2:
                            e.tensor_tensor(r2, kn3[:, :, 64:128], sbr,
                                            op=ALU.mult), deps=kapp)
                    t3 = op("vector", lambda e, kr3=kr3, r1=r1, r2=r2:
                            e.tensor_tensor(kr3[:, :, 0:64], r1, r2,
                                            op=ALU.add),
                            deps=[t1, t2] + cur.get(f"krb_free{slk}", []))
                    t4 = op("vector", lambda e, kn3=kn3, cb=cb, r1=r1:
                            e.tensor_tensor(r1, kn3[:, :, 64:128], cb,
                                            op=ALU.mult), deps=[t3])
                    t5 = op("vector", lambda e, kn3=kn3, sbr=sbr, r2=r2:
                            e.tensor_tensor(r2, kn3[:, :, 0:64], sbr,
                                            op=ALU.mult), deps=[t3])
                    t6 = op("vector", lambda e, kr3=kr3, r1=r1, r2=r2:
                            e.tensor_tensor(kr3[:, :, 64:128], r1, r2,
                                            op=ALU.subtract), deps=[t4, t5])
                    if kpend[0] is not None:
                        emit_ktr(kpend[0])
                    kpend[0] = (g, t3, t6, slk)
                emit_ktr(kpend[0])
                kpend[0] = None

                mm = None
                for kc in range(NT):
                    mm = op("tensor", lambda e, kc=kc, first=(kc == 0),
                            last=(kc == NT - 1): e.matmul(
                        pb[5][0:NV, 0:KV],
                        smb[:, kc * NV:(kc + 1) * NV],
                        vt[:, kc * KV:(kc + 1) * KV],
                        start=first, stop=last),
                        deps=(bank_dep(5) + [(sv["vector"], sv["vector"].n)])
                        if kc == 0 else [], inc=(kc == NT - 1))
                tvs = op("vector", lambda e: e.tensor_copy(
                    vsuf[0:NV - 1, :], pb[5][0:NV - 1, 0:KV]), deps=[mm])
                set_bank_dep(5, [tvs])
                for gg in range(G):
                    bi = 6 + (gg % 2)
                    mt = op("tensor", lambda e, gg=gg, bi=bi: e.matmul(
                        pb[bi][:, 0:NV],
                        vsuf[0:NV, gg * 128:(gg + 1) * 128],
                        idb[0:NV, 0:NV], start=True, stop=True),
                        deps=[tvs, (DS("res"), DS("res").n)] + bank_dep(bi))
                    tv2 = op("vector", lambda e, gg=gg, bi=bi:
                             e.tensor_scalar_mul(
                        vsufT[:, gg * NV:(gg + 1) * NV], pb[bi][:, 0:NV],
                        EXPM50), deps=[mt])
                    set_bank_dep(bi, [tv2])

                run_phase(dma_waits=[(DS("x0"), DS("x0").n),
                                     (DS("x1"), DS("x1").n)])

            lnTo = sb(sAB, "lnTo", [128, NJ * C], BF16)

            sA1b = ExitStack()
            with sA1b:
                xb2 = [sb(sA1b, f"xb2_{i}", [128, C], F32) for i in range(2)]
                lnbB = sb(sA1b, "lnbB", [128, C], BF16)
                statsB = sb(sA1b, "statsB", [128, 16], F32)
                stgB = sb(sA1b, "stgB", [128, max(C, 512)], F32)

                phase("a1b")
                cur.pop("lnb_free", None)
                cur.pop("xfree0", None)
                cur.pop("xfree1", None)
                for j in range(NJ):
                    buf = xb2[j % 2]
                    ld = dma("sync", buf[:], xo[j * 128:(j + 1) * 128, :],
                             f"x{j % 2}", deps=cur.get(f"xfree{j % 2}", []))
                    tln = emit_ln(buf, ld, lnbB, statsB, stgB)
                    cur[f"xfree{j % 2}"] = [tln]
                    emit_transposes(
                        lambda gr, j=j: lnTo[:, j * C + gr * 512:
                                             j * C + (gr + 1) * 512],
                        tln, 0, 1, lnbB)
                run_phase(dma_waits=[(DS("x0"), DS("x0").n),
                                     (DS("x1"), DS("x1").n)])

            sB = ExitStack()
            with sB:
                wqb = [sb(sB, f"wqb{i}", [128, CC * 512], BF16) for i in range(2)]
                qfb = sb(sB, "qfb", [128, NJ * 512], BF16)
                tsbT = sb(sB, "tsbT", [128, 4 * 512], F32)
                esbT = sb(sB, "esbT", [128, 4 * 512], BF16)
                qnsb = sb(sB, "qnsb", [128, 512], F32)
                qrb = sb(sB, "qrb", [128, 2 * 512], BF16)
                brt1 = sb(sB, "brt1", [128, 256], F32)
                brt2 = sb(sB, "brt2", [128, 256], F32)
                ysb = sb(sB, "ysb", [128, 512], F32)
                ytb = sb(sB, "ytb", [128, 512], BF16)
                bst = sb(sB, "bst", [128, 16], F32)
                zbb = sb(sB, "zbb", [128, 512], F32)
                SBANKS = (0, 1, 4)

                phase("b")
                wq_free = {0: [], 1: []}
                qf_free = []
                pend_fin = [None]

                def emit_finalize(u):
                    j, g, ztok = u["j"], u["g"], u["ztok"]
                    cnt_add = float((T - 128 * (2 * j + 2)) * EXPM50)
                    vcol = vsufT[:, g * NV + (2 * j + 2):
                                 g * NV + (2 * j + 3)].broadcast_to([128, 512])
                    v3 = op("vector", lambda e, vcol=vcol: e.tensor_tensor(
                        ysb[:], pb[2][:, 0:512], vcol, op=ALU.add),
                        deps=[ztok] + cur.get("ysb_free", []))
                    set_bank_dep(2, [v3])
                    v1 = op("vector", lambda e, cnt_add=cnt_add:
                            e.tensor_scalar_add(
                        zbb[:], pb[3][:, 0:512], cnt_add),
                        deps=[ztok])
                    set_bank_dep(3, [v1])
                    v2 = op("vector", lambda e: e.reciprocal(
                        zbb[:], zbb[:]), deps=[v1])
                    v4 = op("vector", lambda e: e.tensor_mul(
                        ytb[:], ysb[:], zbb[:]),
                        deps=[v3, v2] + cur.get("ytb_free", []))
                    cur["ysb_free"] = [v4]
                    dma("sync", yt[g * 512:(g + 1) * 512,
                                   j * 128:(j + 1) * 128]
                        .rearrange("(h p) q -> p h q", h=4),
                        ytb[:].rearrange("p (h q) -> p h q", h=4),
                        "yst", deps=[v4])
                    cur["ytb_free"] = [(DS("yst"), DS("yst").n)]

                for g in range(G):
                    wbuf = wqb[g % 2]
                    lw = dma("gpsimd",
                             wbuf[:].rearrange("p (cc m) -> p cc m", cc=CC),
                             wq[g].rearrange("(cc p) m -> p cc m", p=128),
                             f"wq{g % 2}", deps=wq_free[g % 2])
                    qpend = [None]

                    def emit_qtr(u):
                        j2, t3q, t6q, sl = u
                        qtt = []
                        for k in range(4):
                            qtt.append(op("tensor", lambda e, k=k, sl=sl:
                                          e.transpose(
                                pbb[5][:, k * 128:(k + 1) * 128],
                                qrb[:, sl + k * 128:sl + (k + 1) * 128],
                                idb[:]),
                                deps=([t3q, t6q] + bank_dep(5))
                                if k == 0 else []))
                        cur[f"qrb_free{sl}"] = [qtt[-1]] + qtt[:-1]
                        tqe = op("vector", lambda e, j2=j2: e.tensor_copy(
                            qfb[:, j2 * 512:(j2 + 1) * 512], pbb[5][:, 0:512]),
                            deps=[qtt[-1]] + qtt[:-1] + qf_free)
                        set_bank_dep(5, [tqe])
                        cur[f"qf_last{j2}"] = tqe

                    for j in range(NJ):
                        bq = pb[6 + (j % 2)]
                        mmq = None
                        dep0 = [lw] + bank_dep(6 + (j % 2))
                        for cc in range(CC):
                            mmq = op("tensor", lambda e, j=j, cc=cc, bq=bq,
                                     wbuf=wbuf,
                                     first=(cc == 0), last=(cc == CC - 1):
                                     e.matmul(
                                bq[:, 0:512],
                                lnTo[:, j * C + cc * 128:j * C + (cc + 1) * 128],
                                wbuf[:, cc * 512:(cc + 1) * 512],
                                start=first, stop=last),
                                deps=dep0 if cc == 0 else [],
                                inc=(cc == CC - 1))
                        qb0 = (j % 2) * 8
                        qs = []
                        for k in range(4):
                            qs.append(op("scalar", lambda e, k=k, bq=bq,
                                         qb0=qb0: e.activation(
                                tsbT[:, k * 128:(k + 1) * 128],
                                bq[:, k * 128:(k + 1) * 128],
                                ACT.Square,
                                accum_out=bst[:, qb0 + k:qb0 + k + 1]),
                                deps=[mmq]))
                        tq = op("scalar", lambda e, qb0=qb0: e.activation(
                            bst[:, qb0:qb0 + 4], bst[:, qb0:qb0 + 4], ACT.Sqrt,
                            scale=1.0 / 128.0), deps=qs)
                        tq = op("vector", lambda e, qb0=qb0: e.tensor_scalar_add(
                            bst[:, qb0:qb0 + 4], bst[:, qb0:qb0 + 4], 1e-6),
                            deps=[tq])
                        tq = op("vector", lambda e, qb0=qb0: e.reciprocal(
                            bst[:, qb0:qb0 + 4], bst[:, qb0:qb0 + 4]),
                            deps=[tq])
                        qapp = []
                        for k in range(4):
                            qapp.append(op("vector", lambda e, k=k, bq=bq,
                                           qb0=qb0: e.scalar_tensor_tensor(
                                qnsb[:, k * 128:(k + 1) * 128],
                                bq[:, k * 128:(k + 1) * 128],
                                bst[:, qb0 + k:qb0 + k + 1], qnwr[:],
                                op0=ALU.mult, op1=ALU.mult),
                                deps=[tq, mmq]))
                        set_bank_dep(6 + (j % 2), qapp)
                        sl = (j % 2) * 512
                        qn3 = qnsb[:].rearrange("p (h z) -> p h z", h=4)
                        qr3 = qrb[:, sl:sl + 512].rearrange(
                            "p (h z) -> p h z", h=4)
                        cbo = cosob[:, j * 64:(j + 1) * 64].unsqueeze(1) \
                            .broadcast_to([128, 4, 64])
                        sbo = sinob[:, j * 64:(j + 1) * 64].unsqueeze(1) \
                            .broadcast_to([128, 4, 64])
                        r1 = brt1[:].rearrange("p (h z) -> p h z", h=4)
                        r2 = brt2[:].rearrange("p (h z) -> p h z", h=4)
                        t1 = op("vector", lambda e, qn3=qn3, cbo=cbo, r1=r1:
                                e.tensor_tensor(r1, qn3[:, :, 0:64], cbo,
                                                op=ALU.mult), deps=qapp)
                        t2 = op("vector", lambda e, qn3=qn3, sbo=sbo, r2=r2:
                                e.tensor_tensor(r2, qn3[:, :, 64:128], sbo,
                                                op=ALU.mult), deps=qapp)
                        t3 = op("vector", lambda e, qr3=qr3, r1=r1, r2=r2:
                                e.tensor_tensor(qr3[:, :, 0:64], r1, r2,
                                                op=ALU.add),
                                deps=[t1, t2] + cur.get(f"qrb_free{sl}", []))
                        t4 = op("vector", lambda e, qn3=qn3, cbo=cbo, r1=r1:
                                e.tensor_tensor(r1, qn3[:, :, 64:128], cbo,
                                                op=ALU.mult), deps=[t3])
                        t5 = op("vector", lambda e, qn3=qn3, sbo=sbo, r2=r2:
                                e.tensor_tensor(r2, qn3[:, :, 0:64], sbo,
                                                op=ALU.mult), deps=[t3])
                        t6 = op("vector", lambda e, qr3=qr3, r1=r1, r2=r2:
                                e.tensor_tensor(qr3[:, :, 64:128], r1, r2,
                                                op=ALU.subtract), deps=[t4, t5])
                        if qpend[0] is not None:
                            emit_qtr(qpend[0])
                        qpend[0] = (j, t3, t6, sl)
                    emit_qtr(qpend[0])
                    qpend[0] = None
                    wq_free[g % 2] = [(sv["tensor"], sv["tensor"].n)]

                    for j in range(NJ):
                        nk = 2 * j + 2
                        qdep = [cur[f"qf_last{j}"]]
                        stoks = {}
                        ttoks = {}
                        mtoks = {}
                        etoks = {}
                        ztoks = {}

                        def do_S(kc):
                            bi = SBANKS[kc % 3]
                            stoks[kc] = op("tensor", lambda e, kc=kc, bi=bi,
                                           j=j, g=g: e.matmul(
                                pb[bi][:, 0:512],
                                kf[:, g * T + kc * 128:g * T + (kc + 1) * 128],
                                qfb[:, j * 512:(j + 1) * 512],
                                start=True, stop=True),
                                deps=qdep + bank_dep(bi))

                        def do_T(kc):
                            bi = SBANKS[kc % 3]
                            sl = tsbT[:, (kc % 4) * 512:(kc % 4) * 512 + 512]
                            tt = op("scalar", lambda e, bi=bi, sl=sl:
                                    e.activation(
                                sl, pb[bi][:, 0:512], ACT.Tanh,
                                scale=SCALE_IN),
                                deps=[stoks[kc]])
                            set_bank_dep(bi, [tt])
                            ttoks[kc] = tt
                            if kc >= nk - 2:
                                tmt = (kc - (nk - 2))
                                sl3 = sl.rearrange("p (h q) -> p h q", h=4)
                                cm3 = cmsk[:, tmt * 128:(tmt + 1) * 128] \
                                    .unsqueeze(1).broadcast_to([128, 4, 128])
                                mtoks[kc] = op("vector", lambda e, sl3=sl3,
                                               cm3=cm3: e.tensor_tensor(
                                    sl3, sl3, cm3, op=ALU.min), deps=[tt])

                        def do_E(pp):
                            base = (pp % 2) * 1024
                            sbase = (2 * pp % 4) * 512
                            deps = [ttoks[2 * pp], ttoks[2 * pp + 1]]
                            if 2 * pp in mtoks:
                                deps.append(mtoks[2 * pp])
                            if 2 * pp + 1 in mtoks:
                                deps.append(mtoks[2 * pp + 1])
                            deps += cur.get(f"esb_free{pp % 2}", [])
                            etoks[pp] = op("scalar", lambda e, base=base,
                                           sbase=sbase: e.activation(
                                esbT[:, base:base + 1024],
                                tsbT[:, sbase:sbase + 1024],
                                ACT.Exp, scale=50.0), deps=deps)

                        def do_PVZ(kc):
                            pp = kc // 2
                            off = (pp % 2) * 1024 + (kc % 2) * 512
                            op("tensor", lambda e, kc=kc, off=off, g=g,
                               first=(kc == 0), last=(kc == nk - 1): e.matmul(
                                pb[2][:, 0:512],
                                vt[:, kc * KV + g * 128:kc * KV + (g + 1) * 128],
                                esbT[:, off:off + 512],
                                start=first, stop=last),
                                deps=[etoks[pp]] +
                                (bank_dep(2) if kc == 0 else []), inc=False)
                            ztoks[kc] = op("tensor", lambda e, kc=kc, off=off,
                                           first=(kc == 0),
                                           last=(kc == nk - 1): e.matmul(
                                pb[3][:, 0:512], onesm[:],
                                esbT[:, off:off + 512],
                                start=first, stop=last),
                                deps=(bank_dep(3) if kc == 0 else []))
                            if kc % 2 == 1:
                                cur[f"esb_free{pp % 2}"] = [ztoks[kc]]

                        do_S(0)
                        do_T(0)
                        do_S(1)
                        do_T(1)
                        do_E(0)
                        if pend_fin[0] is not None:
                            emit_finalize(pend_fin[0])
                        for pp in range(nk // 2):
                            for kc2 in (2 * pp + 2, 2 * pp + 3):
                                if kc2 < nk:
                                    do_S(kc2)
                                    do_T(kc2)
                            if pp + 1 < nk // 2:
                                do_E(pp + 1)
                            do_PVZ(2 * pp)
                            do_PVZ(2 * pp + 1)
                        pend_fin[0] = dict(j=j, g=g, ztok=ztoks[nk - 1])
                    qf_free = [(sv["tensor"], sv["tensor"].n)]
                if pend_fin[0] is not None:
                    emit_finalize(pend_fin[0])
                    pend_fin[0] = None
                run_phase(dma_waits=[(DS("yst"), DS("yst").n),
                                     (DS("wq0"), DS("wq0").n),
                                     (DS("wq1"), DS("wq1").n)])

        sB2 = ExitStack()
        with sB2:
            TBT = min(4, NJ)
            TBW = TBT * 128
            wob = sb(sB2, "wob", [128, CC * C], BF16)
            ytl = [sb(sB2, f"ytl{i}", [128, CC * TBW], BF16) for i in range(2)]
            afb = sb(sB2, "afb", [128, CC * TBW], F32)
            xob = [sb(sB2, f"xob{i}", [128, C], F32) for i in range(2)]
            bor = sb(sB2, "bor", [128, C], F32)
            xbo = sb(sB2, "xbo", [128, C], F32)
            x1s = [sb(sB2, f"x1s{i}", [128, C], F32) for i in range(2)]
            idb2 = sb(sB2, "idb2", [128, 128], BF16)
            idf2 = sb(sB2, "idf2", [128, 128], F32)

            bank_deps = {i: [] for i in range(8)}

            def bank_dep(i):
                return bank_deps[i]

            def set_bank_dep(i, toks):
                bank_deps[i] = list(toks)

            phase("b2")
            lwo = dma("gpsimd", wob[:].rearrange("p (jj c) -> p jj c", jj=CC),
                      wo.rearrange("(jj p) c -> p jj c", p=128), "wo")
            dma("sync", bor[:], bo[None, :].broadcast_to([128, C]), "bo")
            dma("sync", idb2[:], idb_in, "bo")
            dma("sync", idf2[:], idf_in, "bo")
            ytw = (DS("yst"), DS("yst").n)
            bodep = (DS("bo"), DS("bo").n)
            ytl_free = {0: [], 1: []}
            x1_free = {0: [], 1: []}
            xo_free = {0: [], 1: []}
            for tb in range(NJ // TBT):
                buf = ytl[tb % 2]
                ldy = dma("sync", buf[:].rearrange("p (jj t) -> p jj t", jj=CC),
                          yt[:, tb * TBW:(tb + 1) * TBW].rearrange(
                              "(jj p) t -> p jj t", p=128),
                          f"ytl{tb % 2}", deps=[ytw] + ytl_free[tb % 2])
                afdone = []
                for cb in range(CC):
                    bk = pb[cb % 2]
                    mmo = None
                    for jj in range(CC):
                        mmo = op("tensor", lambda e, cb=cb, jj=jj, bk=bk,
                                 buf=buf, first=(jj == 0),
                                 last=(jj == CC - 1): e.matmul(
                            bk[:, 0:TBW],
                            wob[:, jj * C + cb * 128:jj * C + (cb + 1) * 128],
                            buf[:, jj * TBW:(jj + 1) * TBW],
                            start=first, stop=last),
                            deps=([ldy, lwo] + bank_dep(cb % 2))
                            if jj == 0 else [], inc=(jj == CC - 1))
                    tev = op("vector", lambda e, cb=cb, bk=bk: e.tensor_copy(
                        afb[:, cb * TBW:(cb + 1) * TBW], bk[:, 0:TBW]),
                        deps=[mmo])
                    set_bank_dep(cb % 2, [tev])
                    afdone.append(tev)
                ytl_free[tb % 2] = [afdone[-1]]
                for q in range(TBT):
                    tt = tb * TBT + q
                    ldx = dma("sync", xob[tt % 2][:],
                              xo[tt * 128:(tt + 1) * 128, :],
                              f"xo{tt % 2}", deps=xo_free[tt % 2])
                    txb = op("vector", lambda e, tt=tt: e.tensor_add(
                        xbo[:], xob[tt % 2][:], bor[:]),
                        deps=[ldx, bodep] + cur.get("xbo_free", []))
                    trs = []
                    for cb in range(CC):
                        bk = pb[4 + ((cb // 4) % 2)]
                        ttr = op("tensor", lambda e, cb=cb, q=q, bk=bk:
                                 e.transpose(
                            bk[:, (cb % 4) * 128:(cb % 4 + 1) * 128],
                            afb[:, cb * TBW + q * 128:cb * TBW + (q + 1) * 128],
                            idf2[:]),
                            deps=(afdone + bank_dep(4 + ((cb // 4) % 2)) +
                                  [bodep])
                            if cb % 4 == 0 else [],
                            inc=(cb % 4 == 3))
                        if cb % 4 == 3:
                            tadd = op("vector", lambda e, cb=cb, tt=tt, bk=bk:
                                      e.tensor_add(
                                x1s[tt % 2][:, (cb - 3) * 128:(cb + 1) * 128],
                                bk[:, 0:512],
                                xbo[:, (cb - 3) * 128:(cb + 1) * 128]),
                                deps=[ttr, txb] + x1_free[tt % 2])
                            set_bank_dep(4 + ((cb // 4) % 2), [tadd])
                            trs.append(tadd)
                    cur["xbo_free"] = [trs[-1]]
                    std = dma("sync", out[tt * 128:(tt + 1) * 128, :],
                              x1s[tt % 2][:], f"x1st{tt % 2}", deps=trs)
                    x1_free[tt % 2] = [std]
                    xo_free[tt % 2] = [txb]
            run_phase(dma_waits=[(DS("x1st0"), DS("x1st0").n),
                                 (DS("x1st1"), DS("x1st1").n),
                                 (DS("wo"), DS("wo").n),
                                 (DS("ytl0"), DS("ytl0").n),
                                 (DS("ytl1"), DS("ytl1").n)])

        sF = ExitStack()
        with sF:
            hb = sb(sF, "hb", [128, NF * BT], BF16)
            adb = sb(sF, "adb", [128, NF * 72], BF16)
            bdb = sb(sF, "bdb", [72, C], BF16)
            rad = sb(sF, "rad", [72, BT], BF16)
            agub = sb(sF, "agub", [128, CC * 152], BF16)
            bgb = sb(sF, "bgb", [72, DFF], BF16)
            bub = sb(sF, "bub", [72, DFF], BF16)
            wsc = sb(sF, "wsc", [128, NTT * 8], F32)
            ebr = sb(sF, "ebr", [128, 8], F32)
            idf3 = sb(sF, "idf3", [128, 128], F32)
            idb3 = sb(sF, "idb3", [128, 128], BF16)

            first_res = [True]
            for blk in range(NB):
                sF1 = ExitStack()
                with sF1:
                    ln2f = sb(sF1, f"ln2f_{blk}", [128, CC * BT], BF16)
                    wgub = [sb(sF1, f"wgub{i}_{blk}", [128, 2 * CC * 128], BF16)
                            for i in range(2)]
                    x1l = [sb(sF1, f"x1l{i}_{blk}", [128, C], F32) for i in range(2)]
                    lnb2 = sb(sF1, f"lnb2_{blk}", [128, C], BF16)
                    sg = [sb(sF1, f"sg{i}_{blk}", [128, NSP], F32) for i in range(2)]
                    raf = sb(sF1, f"raf_{blk}", [128, 72], F32)
                    ragf = sb(sF1, f"ragf_{blk}", [72, BT], BF16)
                    rauf = sb(sF1, f"rauf_{blk}", [72, BT], BF16)
                    fst = sb(sF1, f"fst_{blk}", [128, 24], F32)
                    scrw = sb(sF1, f"scrw_{blk}", [128, 48], F32)

                    bank_deps = {i: [] for i in range(8)}
                    phase(f"f1_{blk}")
                    if first_res[0]:
                        dma("gpsimd", agub[:].rearrange(
                            "p (cc r) -> p cc r", cc=CC),
                            agu.rearrange("(cc p) r -> p cc r", p=128), "fres")
                        dma("gpsimd", bgb[:], bg, "fres")
                        dma("gpsimd", bub[:], bu, "fres")
                        dma("gpsimd", adb[:].rearrange(
                            "p (fc r) -> p fc r", fc=NF),
                            ad.rearrange("(fc p) r -> p fc r", p=128), "fres")
                        dma("gpsimd", bdb[:], bd, "fres")
                        dma("sync", ebr[:],
                            ebias[None, :].broadcast_to([128, 8]), "fres2")
                        dma("sync", idf3[:], idf_in, "fres2")
                        dma("sync", idb3[:], idb_in, "fres2")
                        first_res[0] = False
                    fdep = [(DS("fres"), DS("fres").n),
                            (DS("fres2"), DS("fres2").n)]

                    rapend = [None]

                    def emit_ra(u):
                        tt2, bc2, mr2, tws2, tsg2 = u
                        ra_toks = []
                        for (cb0, dst) in ((0, ragf), (72, rauf)):
                            t1 = op("vector", lambda e, cb0=cb0, tt2=tt2,
                                    bc2=bc2: e.tensor_tensor(
                                raf[:, 0:64].rearrange("p (x r) -> p x r", x=8),
                                pb[bc2][:, cb0:cb0 + 64].rearrange(
                                    "p (x r) -> p x r", x=8),
                                wsc[:, tt2 * 8:(tt2 + 1) * 8].unsqueeze(2)
                                .broadcast_to([128, 8, 8]),
                                op=ALU.mult), deps=[mr2, tws2])
                            t2 = op("vector", lambda e, cb0=cb0, bc2=bc2:
                                    e.tensor_scalar_mul(
                                raf[:, 64:72], pb[bc2][:, cb0 + 64:cb0 + 72],
                                0.125), deps=[t1])
                            ra_toks += [t1, t2]
                            ttr = op("tensor", lambda e: e.transpose(
                                pb[5][0:72, 0:128], raf[:, 0:72], idf3[:]),
                                deps=[t1, t2, fdep[1]] + bank_dep(5))
                            tre = op("vector", lambda e, dst=dst, tt2=tt2:
                                     e.tensor_copy(
                                dst[0:72, tt2 * 128:(tt2 + 1) * 128],
                                pb[5][0:72, 0:128]), deps=[ttr])
                            set_bank_dep(5, [tre])
                            cur[f"ra_{id(dst)}_{tt2}"] = tre
                        set_bank_dep(bc2, ra_toks + [tsg2])

                    x1w = [(DS("x1st0"), DS("x1st0").n), (DS("x1st1"), DS("x1st1").n)]
                    x1_free2 = {0: [], 1: []}
                    ln_done = []
                    for tt in range(NTT):
                        trow = blk * BT + tt * 128
                        buf = x1l[tt % 2]
                        ld = dma("sync", buf[:], out[trow:trow + 128, :],
                                 f"x1l{tt % 2}", deps=x1w + x1_free2[tt % 2])
                        lnb2_free = cur.get("lnb2_free", [])
                        NSS2 = C // 512
                        jview = ln2f[:].rearrange("p (cc t) -> p cc t", cc=CC)
                        sqs = []
                        for i in range(NSS2):
                            sqs.append(op("scalar", lambda e, buf=buf, i=i,
                                          tt=tt, jview=jview: e.activation(
                                jview[:, 4 * i:4 * i + 4,
                                      tt * 128:(tt + 1) * 128],
                                buf[:, i * 512:(i + 1) * 512],
                                ACT.Square, accum_out=fst[:, i:i + 1]),
                                deps=[ld] + (lnb2_free if i == 0 else [])))
                        tsum = op("vector", lambda e, buf=buf: e.reduce_sum(
                            fst[:, 4:5], buf[:], axis=AX.X), deps=[ld])
                        tmu = op("vector", lambda e: e.tensor_scalar_mul(
                            fst[:, 5:6], fst[:, 4:5], -1.0 / C), deps=[tsum])
                        tm2 = op("vector", lambda e: e.tensor_mul(
                            fst[:, 6:7], fst[:, 5:6], fst[:, 5:6]), deps=[tmu])
                        tex = op("vector", lambda e: e.reduce_sum(
                            fst[:, 7:8], fst[:, 0:NSS2], axis=AX.X), deps=sqs)
                        tvar = op("vector", lambda e: e.tensor_scalar(
                            fst[:, 8:9], fst[:, 7:8], 1.0 / C, fst[:, 6:7],
                            op0=ALU.mult, op1=ALU.subtract), deps=[tex, tm2])
                        tsd = op("scalar", lambda e: e.activation(
                            fst[:, 9:10], fst[:, 8:9], ACT.Sqrt,
                            bias=cvr[:, 0:1]), deps=[tvar])
                        trs = op("vector", lambda e: e.reciprocal(
                            fst[:, 10:11], fst[:, 9:10]), deps=[tsd])
                        tln = op("vector", lambda e, buf=buf: e.tensor_scalar(
                            lnb2[:], buf[:], fst[:, 5:6], fst[:, 10:11],
                            op0=ALU.add, op1=ALU.mult),
                            deps=[tmu, trs] + sqs + lnb2_free)
                        x1_free2[tt % 2] = [tln]
                        for grp in range(CC // 4):
                            bi = grp % 2
                            tts = []
                            for q in range(4):
                                cc = grp * 4 + q
                                tts.append(op("tensor",
                                              lambda e, bi=bi, q=q, cc=cc:
                                              e.transpose(
                                    pbb[bi][:, q * 128:(q + 1) * 128],
                                    lnb2[:, cc * 128:(cc + 1) * 128],
                                    idb3[:]),
                                    deps=([tln, fdep[1]] + bank_dep(grp % 2))
                                    if q == 0 else [], inc=(q == 3)))
                            dst = ln2f[:].rearrange(
                                "p (cc t) -> p cc t", cc=CC)
                            tev = op("vector", lambda e, grp=grp, tt=tt,
                                     bi=bi, dst=dst: e.tensor_copy(
                                dst[:, grp * 4:(grp + 1) * 4,
                                    tt * 128:(tt + 1) * 128],
                                pbb[bi][:, 0:512].rearrange(
                                    "p (a z) -> p a z", a=4)),
                                deps=[tts[-1]])
                            bank_deps[grp % 2] = [tev]
                            ln_done.append(tev)
                            cur["lnb2_free"] = [tts[-1]]
                        bc = 6 + (tt % 2)
                        mr = None
                        for cc in range(CC):
                            mr = op("tensor", lambda e, cc=cc, tt=tt, bc=bc,
                                    first=(cc == 0), last=(cc == CC - 1):
                                    e.matmul(
                                pb[bc][:, 0:152],
                                ln2f[:].rearrange("p (cc t) -> p cc t", cc=CC)
                                [:, cc, tt * 128:(tt + 1) * 128],
                                agub[:, cc * 152:(cc + 1) * 152],
                                start=first, stop=last),
                                deps=(ln_done[-4:] + fdep + bank_dep(bc))
                                if cc == 0 else [], inc=(cc == CC - 1))
                        if rapend[0] is not None:
                            emit_ra(rapend[0])
                        tsg = op("scalar", lambda e, bc=bc: e.activation(
                            scrw[:, 0:8], pb[bc][:, 144:152], ACT.Sigmoid),
                            deps=[mr] + cur.get("scrw_free", []))
                        tb_ = op("vector", lambda e: e.tensor_add(
                            scrw[:, 8:16], scrw[:, 0:8], ebr[:]),
                            deps=[tsg, fdep[1]])
                        tm1 = op("vector", lambda e: e.reduce_max(
                            fst[:, 16:17], scrw[:, 8:16], axis=AX.X),
                            deps=[tb_])
                        tk1 = op("vector", lambda e: e.tensor_scalar(
                            scrw[:, 16:24], scrw[:, 8:16], fst[:, 16:17], None,
                            op0=ALU.is_ge), deps=[tm1])
                        ts1 = op("vector", lambda e: e.tensor_mul(
                            scrw[:, 24:32], scrw[:, 16:24], scrw[:, 0:8]),
                            deps=[tk1])
                        ts1r = op("vector", lambda e: e.reduce_sum(
                            fst[:, 17:18], scrw[:, 24:32], axis=AX.X),
                            deps=[ts1])
                        tb2 = op("vector", lambda e: e.scalar_tensor_tensor(
                            scrw[:, 32:40], scrw[:, 16:24], -1e9,
                            scrw[:, 8:16], op0=ALU.mult, op1=ALU.add),
                            deps=[tk1])
                        tm2_ = op("vector", lambda e: e.reduce_max(
                            fst[:, 18:19], scrw[:, 32:40], axis=AX.X),
                            deps=[tb2])
                        tk2 = op("vector", lambda e: e.tensor_scalar(
                            scrw[:, 32:40], scrw[:, 32:40], fst[:, 18:19],
                            None, op0=ALU.is_ge), deps=[tm2_])
                        ts2 = op("vector", lambda e: e.tensor_mul(
                            scrw[:, 40:48], scrw[:, 32:40], scrw[:, 0:8]),
                            deps=[tk2])
                        cur["scrw_free"] = [ts2]
                        ts2r = op("vector", lambda e: e.reduce_sum(
                            fst[:, 19:20], scrw[:, 40:48], axis=AX.X),
                            deps=[ts2])
                        tden = op("vector", lambda e: e.tensor_add(
                            fst[:, 20:21], fst[:, 17:18], fst[:, 19:20]),
                            deps=[ts1r, ts2r])
                        tden = op("vector", lambda e: e.tensor_scalar_add(
                            fst[:, 20:21], fst[:, 20:21], 1e-8), deps=[tden])
                        tinv = op("vector", lambda e: e.reciprocal(
                            fst[:, 21:22], fst[:, 20:21]), deps=[tden])
                        tw1 = op("vector", lambda e: e.tensor_scalar(
                            scrw[:, 16:24], scrw[:, 16:24], fst[:, 17:18],
                            None, op0=ALU.mult), deps=[tk1, ts1r])
                        tw2 = op("vector", lambda e: e.tensor_scalar(
                            scrw[:, 32:40], scrw[:, 32:40], fst[:, 19:20],
                            None, op0=ALU.mult), deps=[tk2, ts2r])
                        twa = op("vector", lambda e: e.tensor_add(
                            scrw[:, 16:24], scrw[:, 16:24], scrw[:, 32:40]),
                            deps=[tw1, tw2])
                        tws = op("vector", lambda e, tt=tt: e.tensor_scalar(
                            wsc[:, tt * 8:(tt + 1) * 8], scrw[:, 16:24],
                            fst[:, 21:22], 0.125, op0=ALU.mult, op1=ALU.mult),
                            deps=[twa, tinv])
                        rapend[0] = (tt, bc, mr, tws, tsg)

                    if rapend[0] is not None:
                        emit_ra(rapend[0])
                        rapend[0] = None
                    ra_done = [(sv["vector"], sv["vector"].n)]
                    wgu_free = {0: [], 1: []}
                    for m in range(NF):
                        wb_ = wgub[m % 2]
                        lw = dma("gpsimd",
                                 wb_[:].rearrange(
                                     "p (a cc z) -> p a cc z", a=2, cc=CC),
                                 wgu[m].rearrange(
                                     "a (cc p) z -> p a cc z", p=128),
                                 f"wgu{m % 2}", deps=wgu_free[m % 2])
                        hdone = []
                        for sp in range(NSB):
                            mgs = mus = None
                            for cc in range(CC):
                                mgs = op("tensor", lambda e, cc=cc, sp=sp,
                                         wb_=wb_, first=(cc == 0):
                                         e.matmul(
                                    pb[0 + sp][:, 0:NSP],
                                    wb_[:, (0 * CC + cc) * 128:
                                        (0 * CC + cc + 1) * 128],
                                    ln2f[:].rearrange(
                                        "p (cc t) -> p cc t", cc=CC)
                                    [:, cc, sp * NSP:(sp + 1) * NSP],
                                    start=first, stop=False),
                                    deps=([lw] + ln_done[-1:] +
                                          bank_dep(0 + sp) + ra_done)
                                    if cc == 0 else [], inc=False)
                                mus = op("tensor", lambda e, cc=cc, sp=sp,
                                         wb_=wb_, first=(cc == 0):
                                         e.matmul(
                                    pb[2 + sp][:, 0:NSP],
                                    wb_[:, (1 * CC + cc) * 128:
                                        (1 * CC + cc + 1) * 128],
                                    ln2f[:].rearrange(
                                        "p (cc t) -> p cc t", cc=CC)
                                    [:, cc, sp * NSP:(sp + 1) * NSP],
                                    start=first, stop=False),
                                    deps=bank_dep(2 + sp) if cc == 0 else [],
                                    inc=False)
                            mgs = op("tensor", lambda e, m=m, sp=sp: e.matmul(
                                pb[0 + sp][:, 0:NSP],
                                bgb[:, m * 128:(m + 1) * 128],
                                ragf[0:72, sp * NSP:(sp + 1) * NSP],
                                start=False, stop=True),
                                deps=[cur[f"ra_{id(ragf)}_{t}"]
                                      for t in range(NTT)][-1:])
                            mus = op("tensor", lambda e, m=m, sp=sp: e.matmul(
                                pb[2 + sp][:, 0:NSP],
                                bub[:, m * 128:(m + 1) * 128],
                                rauf[0:72, sp * NSP:(sp + 1) * NSP],
                                start=False, stop=True),
                                deps=[cur[f"ra_{id(rauf)}_{t}"]
                                      for t in range(NTT)][-1:])
                            tsl = op("scalar", lambda e, sp=sp: e.activation(
                                sg[sp][:], pb[0 + sp][:, 0:NSP], ACT.Silu),
                                deps=[mgs] + cur.get(f"sg_free{sp}", []))
                            set_bank_dep(0 + sp, [tsl])
                            th = op("vector", lambda e, m=m, sp=sp:
                                    e.tensor_mul(
                                hb[:, m * BT + sp * NSP:
                                   m * BT + (sp + 1) * NSP],
                                sg[sp][:], pb[2 + sp][:, 0:NSP]),
                                deps=[tsl, mus])
                            set_bank_dep(2 + sp, [th])
                            cur[f"sg_free{sp}"] = [th]
                            hdone.append(th)
                        wgu_free[m % 2] = hdone
                        cur[f"h_{m}"] = hdone
                    run_phase(dma_waits=[
                        (DS("wgu0"), DS("wgu0").n), (DS("wgu1"), DS("wgu1").n),
                        (DS("fres"), DS("fres").n),
                        (DS(f"x1l0"), DS("x1l0").n),
                        (DS(f"x1l1"), DS("x1l1").n)])

                sF2 = ExitStack()
                with sF2:
                    wdb = [sb(sF2, f"wdb{i}_{blk}", [128, NF * 128], BF16)
                           for i in range(2)]
                    fc2 = sb(sF2, f"fc2_{blk}", [128, NSP], F32)
                    ftok = sb(sF2, f"ftok_{blk}", [128, (NTT // NSB) * C], F32)
                    x1l2 = [sb(sF2, f"x1l2{i}_{blk}", [128, C], F32)
                            for i in range(2)]
                    raf2 = sb(sF2, f"raf2_{blk}", [128, 72], F32)

                    bank_deps = {i: [] for i in range(8)}
                    phase(f"f2_{blk}")
                    hdep = [(sv["vector"], sv["vector"].n)]
                    for tt in range(NTT):
                        mm = None
                        for fc in range(NF):
                            mm = op("tensor", lambda e, fc=fc, tt=tt,
                                    first=(fc == 0), last=(fc == NF - 1):
                                    e.matmul(
                                pb[6][:, 0:72],
                                hb[:, fc * BT + tt * 128:
                                   fc * BT + (tt + 1) * 128],
                                adb[:, fc * 72:(fc + 1) * 72],
                                start=first, stop=last),
                                deps=(hdep + bank_dep(6)) if fc == 0 else [],
                                inc=(fc == NF - 1))
                        t1 = op("vector", lambda e, tt=tt: e.tensor_tensor(
                            raf2[:, 0:64].rearrange("p (x r) -> p x r", x=8),
                            pb[6][:, 0:64].rearrange("p (x r) -> p x r", x=8),
                            wsc[:, tt * 8:(tt + 1) * 8].unsqueeze(2)
                            .broadcast_to([128, 8, 8]), op=ALU.mult),
                            deps=[mm])
                        t2 = op("vector", lambda e: e.tensor_scalar_mul(
                            raf2[:, 64:72], pb[6][:, 64:72], 0.125), deps=[t1])
                        set_bank_dep(6, [t2])
                        ttr = op("tensor", lambda e: e.transpose(
                            pb[7][0:72, 0:128], raf2[:, 0:72], idf3[:]),
                            deps=[t1, t2,
                                  (DS("fres2"), DS("fres2").n)] + bank_dep(7))
                        tre = op("vector", lambda e, tt=tt: e.tensor_copy(
                            rad[0:72, tt * 128:(tt + 1) * 128],
                            pb[7][0:72, 0:128]), deps=[ttr])
                        set_bank_dep(7, [tre])
                        cur[f"rad_{tt}"] = tre
                    rad_done = [(sv["vector"], sv["vector"].n)]

                    wd_free = {0: [], 1: []}
                    x1_free3 = {0: [], 1: []}
                    out_free = {0: [], 1: []}
                    for half in range(NSB):
                        ft_done = []
                        for cb in range(CC):
                            wb_ = wdb[cb % 2]
                            lw = dma("gpsimd",
                                     wb_[:].rearrange(
                                         "p (fc z) -> p fc z", fc=NF),
                                     wd[cb].rearrange(
                                         "(fc p) z -> p fc z", p=128),
                                     f"wd{cb % 2}", deps=wd_free[cb % 2])
                            mm = None
                            for fc in range(NF):
                                mm = op("tensor", lambda e, fc=fc, cb=cb,
                                        half=half, wb_=wb_, first=(fc == 0):
                                        e.matmul(
                                    pb[0 + (cb % 2)][:, 0:NSP],
                                    wb_[:, fc * 128:(fc + 1) * 128],
                                    hb[:, fc * BT + half * NSP:
                                       fc * BT + (half + 1) * NSP],
                                    start=first, stop=False),
                                    deps=([lw] + hdep + bank_dep(0 + (cb % 2)))
                                    if fc == 0 else [], inc=False)
                            mm = op("tensor", lambda e, cb=cb, half=half:
                                    e.matmul(
                                pb[0 + (cb % 2)][:, 0:NSP],
                                bdb[:, cb * 128:(cb + 1) * 128],
                                rad[0:72, half * NSP:(half + 1) * NSP],
                                start=False, stop=True),
                                deps=rad_done)
                            tev = op("vector", lambda e, cb=cb: e.tensor_copy(
                                fc2[:], pb[0 + (cb % 2)][:, 0:NSP]),
                                deps=[mm] + cur.get("fc2_free", []))
                            set_bank_dep(0 + (cb % 2), [tev])
                            wd_free[cb % 2] = [tev]
                            nq = NSP // 128
                            tprev = []
                            for q in range(nq):
                                bk = pb[4 + (q % 2)]
                                ttr = op("tensor", lambda e, q=q, bk=bk:
                                         e.transpose(
                                    bk[:, 0:128],
                                    fc2[:, q * 128:(q + 1) * 128], idf3[:]),
                                    deps=[tev,
                                          (DS("fres2"), DS("fres2").n)] +
                                    bank_dep(4 + (q % 2)))
                                tfe = op("vector", lambda e, q=q, cb=cb,
                                         bk=bk: e.tensor_copy(
                                    ftok[:, q * C + cb * 128:
                                         q * C + (cb + 1) * 128],
                                    bk[:, 0:128]), deps=[ttr])
                                set_bank_dep(4 + (q % 2), [tfe])
                                tprev.append(tfe)
                            ft_done.extend(tprev)
                            cur["fc2_free"] = tprev
                        for q in range(NSP // 128):
                            tt = half * (NSP // 128) + q
                            trow = blk * BT + tt * 128
                            ld = dma("sync", x1l2[q % 2][:],
                                     out[trow:trow + 128, :],
                                     f"x1f{q % 2}",
                                     deps=[(DS("x1st0"), DS("x1st0").n),
                                           (DS("x1st1"), DS("x1st1").n)] +
                                     x1_free3[q % 2])
                            tad = op("vector", lambda e, q=q: e.tensor_add(
                                x1l2[q % 2][:], ftok[:, q * C:(q + 1) * C],
                                x1l2[q % 2][:]),
                                deps=[ld] + ft_done)
                            std = dma("sync", out[trow:trow + 128, :],
                                      x1l2[q % 2][:], f"outst{q % 2}", deps=[tad])
                            x1_free3[q % 2] = [std]
                    run_phase(dma_waits=[
                        (DS("outst0"), DS("outst0").n),
                        (DS("outst1"), DS("outst1").n),
                        (DS("wd0"), DS("wd0").n), (DS("wd1"), DS("wd1").n),
                        (DS("x1f0"), DS("x1f0").n),
                        (DS("x1f1"), DS("x1f1").n)])

    return nc



def _prep_inputs(ins, cfg):
    C = cfg["C"]; T = cfg["T"]; TOWN = cfg["TOWN"]; H = cfg["H"]; G = cfg["G"]
    DFF = cfg["DFF"]
    CC = C // 128; NT = T // 128; NJ = TOWN // 128; KV = G * 128
    HP = H // 2; NF = DFF // 128; NV = NT + 1

    f32 = np.float32
    import ml_dtypes
    bf16 = ml_dtypes.bfloat16

    wqT = np.ascontiguousarray(ins["L0_attn_Wq"].T)
    wqp = np.ascontiguousarray(
        wqT.reshape(C, G, 512).transpose(1, 0, 2))
    wkT = np.ascontiguousarray(ins["L0_attn_Wk"].T)
    wvT = np.ascontiguousarray(ins["L0_attn_Wv"].T)
    woT = np.ascontiguousarray(ins["L0_attn_Wo"].T)

    Wg = ins["L0_ffn_W_gate"]; Wu = ins["L0_ffn_W_up"]; Wd = ins["L0_ffn_W_down"]
    wgu = np.ascontiguousarray(np.stack(
        [Wg.reshape(C, NF, 128).transpose(1, 0, 2),
         Wu.reshape(C, NF, 128).transpose(1, 0, 2)], axis=1))
    wdp = np.ascontiguousarray(
        Wd.reshape(DFF, CC, 128).transpose(1, 0, 2))

    agc = np.concatenate(
        [ins["L0_ffn_A_gate"].transpose(1, 0, 2).reshape(C, 64),
         ins["L0_ffn_sA_gate"]], axis=1)
    auc = np.concatenate(
        [ins["L0_ffn_A_up"].transpose(1, 0, 2).reshape(C, 64),
         ins["L0_ffn_sA_up"]], axis=1)
    aguc = np.ascontiguousarray(np.concatenate(
        [agc, auc, ins["L0_ffn_W_router"]], axis=1))
    adc = np.ascontiguousarray(np.concatenate(
        [ins["L0_ffn_A_down"].transpose(1, 0, 2).reshape(DFF, 64),
         ins["L0_ffn_sA_down"]], axis=1))
    bgc = np.ascontiguousarray(np.concatenate(
        [ins["L0_ffn_B_gate"].reshape(64, DFF), ins["L0_ffn_sB_gate"]], axis=0))
    buc = np.ascontiguousarray(np.concatenate(
        [ins["L0_ffn_B_up"].reshape(64, DFF), ins["L0_ffn_sB_up"]], axis=0))
    bdc = np.ascontiguousarray(np.concatenate(
        [ins["L0_ffn_B_down"].reshape(64, C), ins["L0_ffn_sB_down"]], axis=0))

    inv_freq = 1.0 / 10000.0 ** (np.arange(0, 128, 2, dtype=np.float64) / 128)
    tt = np.arange(T, dtype=np.float64)
    fr = np.outer(tt, inv_freq)
    cosg = np.cos(fr).astype(f32)
    sing = np.sin(fr).astype(f32)

    sm = (np.arange(T)[:, None] >= 128 * np.arange(NV)[None, :]).astype(bf16)

    ident = np.eye(128, dtype=f32)
    identb = np.eye(128).astype(bf16)

    x = ins["L0_attn_x"]
    B = x.shape[0]
    maps = []
    for core in range(B * 2):
        b, s = core // 2, core % 2
        xb = np.ascontiguousarray(x[b])
        own = np.ascontiguousarray(
            x[b].reshape(NT, 128, C)[s::2].reshape(TOWN, C))
        coso = np.ascontiguousarray(
            cosg.reshape(NT, 128, 64)[s::2].reshape(TOWN, 64))
        sino = np.ascontiguousarray(
            sing.reshape(NT, 128, 64)[s::2].reshape(TOWN, 64))
        kp = np.arange(128)[:, None]
        t_ = np.arange(256)[None, :] // 128
        qq = np.arange(256)[None, :] % 128
        cm = np.where(t_ * 128 + kp <= qq + 128 * s, 2.0, -1.0).astype(f32)
        maps.append(dict(
            xf=xb, xo=own, wq=wqp, wk=wkT, wv=wvT, wo=woT,
            qnw=ins["L0_attn_q_norm_w"], knw=ins["L0_attn_k_norm_w"],
            bo=ins["L0_attn_bo"],
            ebias=ins["L0_ffn_expert_bias"], wgu=wgu, wd=wdp,
            agu=aguc, ad=adc, bg=bgc, bu=buc, bd=bdc,
            cosg=cosg, sing=sing, coso=coso, sino=sino,
            cmaskin=cm, smask=sm,
            cvals=np.array([1e-5, -50.0], np.float32),
            idf_in=ident, idb_in=identb))
    return maps


def kernel(**inputs):
    cfg = FULL
    maps = _prep_inputs(inputs, cfg)
    nc = build(cfg)
    res = run_bass_kernel_spmd(nc, maps, core_ids=list(range(8)))
    T = cfg["T"]; C = cfg["C"]; NT = T // 128; NJ = cfg["TOWN"] // 128
    B = inputs["L0_attn_x"].shape[0]
    outp = np.empty((B, T, C), np.float32)
    for core in range(8):
        b, s = core // 2, core % 2
        outp[b].reshape(NT, 128, C)[s::2] = \
            res.results[core]["out"].reshape(NJ, 128, C)
    return outp


# version 16 (final)
# speedup vs baseline: 1.0672x; 1.0672x over previous
import math
from contextlib import ExitStack

import numpy as np

import concourse.bass as bass
import concourse.mybir as mybir
from concourse.bass_utils import run_bass_kernel_spmd

F32 = mybir.dt.float32
BF16 = mybir.dt.bfloat16
AX = mybir.AxisListType
ALU = mybir.AluOpType
ACT = mybir.ActivationFunctionType

ENGINES = ("sync", "gpsimd", "vector", "scalar", "tensor")

FULL = dict(C=2048, T=4096, TOWN=2048, H=16, G=4, DFF=5504, NB=2)


class Sem:
    __slots__ = ("h", "n", "name")

    def __init__(self, es, nc, name):
        self.h = es.enter_context(nc.semaphore(name))
        self.n = 0
        self.name = name


class Rec:

    def __init__(self):
        self.stream = {e: [] for e in ENGINES}

    def add(self, eng, fn, deps, sem, n):
        self.stream[eng].append((fn, tuple(deps), sem, n))


def build(cfg, dbg=False):
    C = cfg["C"]; T = cfg["T"]; TOWN = cfg["TOWN"]; H = cfg["H"]; G = cfg["G"]
    DFF = cfg["DFF"]; NB = cfg["NB"]
    CC = C // 128
    NT = T // 128
    NJ = TOWN // 128
    KV = G * 128
    HP = H // 2
    NF = DFF // 128
    BT = TOWN // NB
    NTT = BT // 128
    NSP = cfg.get("NSP", min(512, BT))
    NSB = BT // NSP
    NV = NT + 1
    GRP = H // G
    SCALE_IN = 1.0 / (50.0 * math.sqrt(128.0))
    EXPM50 = math.exp(-50.0)

    nc = bass.Bass("TRN2", target_bir_lowering=False, debug=False)

    def din(name, shape, dt=F32):
        return nc.dram_tensor(name, shape, dt, kind="ExternalInput").ap()

    xf = din("xf", [T, C])
    xo = din("xo", [TOWN, C])
    wq = din("wq", [G, C, 512])
    wk = din("wk", [C, KV])
    wv = din("wv", [C, KV])
    wo = din("wo", [C, C])
    qnw = din("qnw", [128])
    knw = din("knw", [128])
    bo = din("bo", [C])
    ebias = din("ebias", [8])
    wgu = din("wgu", [NF, 2, C, 128])
    wd = din("wd", [CC, DFF, 128])
    agu = din("agu", [C, 152])
    ad = din("ad", [DFF, 72])
    bg = din("bg", [72, DFF])
    bu = din("bu", [72, DFF])
    bd = din("bd", [72, C])
    cosg = din("cosg", [T, 64])
    sing = din("sing", [T, 64])
    coso = din("coso", [TOWN, 64])
    sino = din("sino", [TOWN, 64])
    cmaskin = din("cmaskin", [128, 256])
    cvals = din("cvals", [2])
    smask = din("smask", [T, NV], BF16)
    idf_in = din("idf_in", [128, 128])
    idb_in = din("idb_in", [128, 128], BF16)

    out = nc.dram_tensor("out", [TOWN, C], F32, kind="ExternalOutput").ap()
    yt = nc.dram_tensor("yt", [H * 128, TOWN], BF16, kind="Internal").ap()

    es = ExitStack()
    with es:
        pb = [es.enter_context(nc.psum_tensor(f"pb{i}", [128, 512], F32))
              for i in range(8)]
        pbb = [t[:].bitcast(BF16) for t in pb]
        sv = {e: Sem(es, nc, "se_" + e) for e in ENGINES}
        ph = Sem(es, nc, "ph")
        dsem = {}

        def DS(name):
            if name not in dsem:
                dsem[name] = Sem(es, nc, "d_" + name)
            return dsem[name]

        cvr = es.enter_context(nc.sbuf_tensor("cvr", [128, 2], F32))
        waited = {e: {} for e in ENGINES}
        cur = {"rec": None}
        n_phases = [0]

        def op(eng, fn, deps=(), inc=True, n=1):
            sem = sv[eng] if inc else None
            if sem is not None:
                sem.n += n
            cur["rec"].add(eng, fn, deps, sem, n)
            return (sem, sem.n) if sem else None

        def dma(q, dst, src, stream, deps=()):
            s = DS(stream)
            s.n += 16
            cur["rec"].add(q, (lambda e, d=dst, sc=src: e.dma_start(d, sc)),
                           deps, s, 16)
            return (s, s.n)

        def phase(name):
            cur["rec"] = Rec()

        def run_phase(dma_waits=()):
            rec = cur["rec"]
            k = n_phases[0]
            for e in ENGINES:
                deps = [(ph, 5 * k)] if k > 0 else []
                rec.stream[e].insert(0, (None, tuple(deps), None, 0))
            for e in ENGINES:
                deps = list(dma_waits) if e == "sync" else []
                ph.n += 1
                rec.stream[e].append(
                    (lambda eng: eng.nop(), tuple(deps), ph, 1))
            with nc.Block() as blk:
                for e in ENGINES:
                    ops = rec.stream[e]

                    def body(engine, ops=ops, e=e):
                        seen = waited[e]
                        for fn, deps, sem, n in ops:
                            for (s, v) in deps:
                                if seen.get(s.name, 0) < v:
                                    engine.wait_ge(s.h, v)
                                    seen[s.name] = v
                            if fn is None:
                                continue
                            ins = fn(engine)
                            if sem is not None:
                                ins.then_inc(sem.h, n)
                    getattr(blk, e)(body)
            n_phases[0] += 1
            cur["rec"] = None

        def sb(scope, name, shape, dt):
            return scope.enter_context(nc.sbuf_tensor(name, shape, dt))

        bank_deps = {i: [] for i in range(8)}

        def bank_dep(i):
            return bank_deps[i]

        def set_bank_dep(i, toks):
            bank_deps[i] = list(toks)

        NSS = C // 512

        sAB = ExitStack()
        with sAB:
            kf = sb(sAB, "kf", [128, G * T], BF16)
            vt = sb(sAB, "vt", [128, NT * KV], BF16)
            vsufT = sb(sAB, "vsufT", [128, G * NV], F32)
            onesm = sb(sAB, "onesm", [128, 128], BF16)
            qnwr = sb(sAB, "qnwr", [128, 128], F32)
            knwr = sb(sAB, "knwr", [128, 128], F32)
            cmsk = sb(sAB, "cmsk", [128, 256], F32)
            cosob = sb(sAB, "cosob", [128, NJ * 64], BF16)
            sinob = sb(sAB, "sinob", [128, NJ * 64], BF16)
            idf = sb(sAB, "idf", [128, 128], F32)
            idb = sb(sAB, "idb", [128, 128], BF16)

            def emit_ln(xtile, dep_load, lnb, stats, stg):
                lnb_free = cur.get("lnb_free", [])
                sqs = []
                for i in range(NSS):
                    sqs.append(op("scalar", lambda e, x=xtile, i=i, stg=stg,
                                  stats=stats: e.activation(
                        stg[:, i * 512:(i + 1) * 512],
                        x[:, i * 512:(i + 1) * 512],
                        ACT.Square, accum_out=stats[:, i:i + 1]),
                        deps=[dep_load] + (lnb_free if i == 0 else [])))
                tsum = op("vector", lambda e, x=xtile, stats=stats:
                          e.reduce_sum(stats[:, 4:5], x[:], axis=AX.X),
                          deps=[dep_load])
                tmu = op("vector", lambda e, stats=stats: e.tensor_scalar_mul(
                    stats[:, 5:6], stats[:, 4:5], -1.0 / C), deps=[tsum])
                tm2 = op("vector", lambda e, stats=stats: e.tensor_mul(
                    stats[:, 6:7], stats[:, 5:6], stats[:, 5:6]), deps=[tmu])
                tex = op("vector", lambda e, stats=stats: e.reduce_sum(
                    stats[:, 7:8], stats[:, 0:NSS], axis=AX.X), deps=sqs)
                tvar = op("vector", lambda e, stats=stats: e.tensor_scalar(
                    stats[:, 8:9], stats[:, 7:8], 1.0 / C, stats[:, 6:7],
                    op0=ALU.mult, op1=ALU.subtract), deps=[tex, tm2])
                tsd = op("scalar", lambda e, stats=stats: e.activation(
                    stats[:, 9:10], stats[:, 8:9], ACT.Sqrt,
                    bias=cvr[:, 0:1]), deps=[tvar])
                trs = op("vector", lambda e, stats=stats: e.reciprocal(
                    stats[:, 10:11], stats[:, 9:10]), deps=[tsd])
                tln = op("vector", lambda e, x=xtile, lnb=lnb, stats=stats:
                         e.tensor_scalar(
                    lnb[:], x[:], stats[:, 5:6], stats[:, 10:11],
                    op0=ALU.add, op1=ALU.mult),
                    deps=[tmu, trs] + sqs + lnb_free)
                return tln

            def emit_transposes(dst_fn, tln, bank0, bank1, lnb):
                toks = []
                last_tt = None
                for grp in range(CC // 4):
                    bi = bank0 if grp % 2 == 0 else bank1
                    tts = []
                    for q in range(4):
                        cc = grp * 4 + q
                        tts.append(op("tensor", lambda e, bi=bi, q=q, cc=cc,
                                      lnb=lnb: e.transpose(
                            pbb[bi][:, q * 128:(q + 1) * 128],
                            lnb[:, cc * 128:(cc + 1) * 128],
                            idb[:]),
                            deps=([tln] + bank_dep(bi)) if q == 0 else []))
                    tev = op("vector", lambda e, bi=bi, g=grp:
                             e.tensor_copy(dst_fn(g), pbb[bi][:, 0:512]),
                             deps=[tts[-1]] + tts[:-1])
                    set_bank_dep(bi, [tev])
                    toks.append(tev)
                    last_tt = tts[-1]
                cur["lnb_free"] = [last_tt]
                return toks

            sA1 = ExitStack()
            with sA1:
                cosgb = sb(sA1, "cosgb", [128, NT * 64], BF16)
                singb = sb(sA1, "singb", [128, NT * 64], BF16)
                wkb = sb(sA1, "wkb", [128, CC * KV], BF16)
                wvb = sb(sA1, "wvb", [128, CC * KV], BF16)
                smb = sb(sA1, "smb", [128, NT * NV], BF16)
                xb = [sb(sA1, f"xb{i}", [128, C], F32) for i in range(2)]
                lnb = sb(sA1, "lnb", [128, C], BF16)
                lnscr = sb(sA1, "lnscr", [128, C], BF16)
                knsb = sb(sA1, "knsb", [128, KV], F32)
                krb = sb(sA1, "krb", [128, 2 * KV], BF16)
                rt1 = sb(sA1, "rt1", [128, 256], F32)
                rt2 = sb(sA1, "rt2", [128, 256], F32)
                stats = sb(sA1, "stats", [128, 16], F32)
                vsuf = sb(sA1, "vsuf", [NV, KV], BF16)
                stg = sb(sA1, "stg", [128, max(C, 512, max(NT, NJ) * 64)], F32)

                phase("p0")
                dma("sync", cvr[:], cvals[None, :].broadcast_to([128, 2]), "res")
                dma("sync", idf[:], idf_in, "res")
                dma("sync", idb[:], idb_in, "res")
                dma("sync", qnwr[:], qnw[None, :].broadcast_to([128, 128]), "res")
                dma("sync", knwr[:], knw[None, :].broadcast_to([128, 128]), "res")
                dma("sync", cmsk[:], cmaskin, "res")
                dma("gpsimd", wkb[:].rearrange("p (cc m) -> p cc m", cc=CC),
                    wk.rearrange("(cc p) m -> p cc m", p=128), "wkv")
                dma("gpsimd", wvb[:].rearrange("p (cc m) -> p cc m", cc=CC),
                    wv.rearrange("(cc p) m -> p cc m", p=128), "wkv")
                dma("gpsimd", smb[:].rearrange("p (t v) -> p t v", t=NT),
                    smask.rearrange("(t p) v -> p t v", p=128), "wkv")
                for (src, dst, nblk) in ((cosg, cosgb, NT), (sing, singb, NT),
                                         (coso, cosob, NJ), (sino, sinob, NJ)):
                    t = dma("sync", stg[:, 0:nblk * 64].rearrange(
                        "p (t v) -> p t v", t=nblk),
                        src.rearrange("(t p) v -> p t v", p=128), "tab",
                        deps=[cur["last_cast"]] if "last_cast" in cur else [])
                    cur["last_cast"] = op(
                        "vector", lambda e, d=dst, s=stg, nb=nblk:
                        e.tensor_copy(d[:, 0:nb * 64], s[:, 0:nb * 64]),
                        deps=[t])
                op("vector", lambda e: e.memset(onesm[:], 1.0))
                op("vector", lambda e: e.memset(vsuf[:, :], 0.0))
                run_phase(dma_waits=[(DS("res"), DS("res").n),
                                     (DS("tab"), DS("tab").n)])

                phase("a1")
                kpend = [None]

                def emit_ktr(u):
                    g2, t3k, t6k, slk = u
                    ktt = []
                    for gg in range(G):
                        ktt.append(op("tensor", lambda e, gg=gg, slk=slk:
                                      e.transpose(
                            pbb[4][:, gg * 128:(gg + 1) * 128],
                            krb[:, slk + gg * 128:slk + (gg + 1) * 128],
                            idb[:]),
                            deps=([t3k, t6k] + bank_dep(4)) if gg == 0 else []))
                    cur[f"krb_free{slk}"] = [ktt[-1]] + ktt[:-1]
                    kdst = kf[:].rearrange("p (g t) -> p g t", g=G)
                    tke = op("vector", lambda e, g2=g2, kdst=kdst:
                             e.tensor_copy(
                        kdst[:, :, g2 * 128:(g2 + 1) * 128],
                        pbb[4][:, 0:G * 128].rearrange(
                            "p (g z) -> p g z", g=G)),
                        deps=[ktt[-1]] + ktt[:-1])
                    set_bank_dep(4, [tke])
                resdep = [(DS("res"), DS("res").n), (DS("wkv"), DS("wkv").n),
                          (sv["vector"], sv["vector"].n)]

                NSS = C // 512

                for g in range(NT):
                    buf = xb[g % 2]
                    ld = dma("sync" if g % 2 == 0 else "gpsimd",
                             buf[:], xf[g * 128:(g + 1) * 128, :],
                             f"x{g % 2}", deps=cur.get(f"xfree{g % 2}", []))
                    tln = emit_ln(buf, ld, lnb, stats, stg)
                    cur[f"xfree{g % 2}"] = [tln]
                    toks = emit_transposes(
                        lambda gr: lnscr[:, gr * 512:(gr + 1) * 512],
                        tln, 0, 1, lnb)
                    kdeps = resdep + toks + bank_dep(2) + bank_dep(3)
                    mmk = mmv = None
                    for cc in range(CC):
                        mmk = op("tensor", lambda e, cc=cc, first=(cc == 0),
                                 last=(cc == CC - 1): e.matmul(
                            pb[2][:, 0:KV],
                            lnscr[:, cc * 128:(cc + 1) * 128],
                            wkb[:, cc * KV:(cc + 1) * KV],
                            start=first, stop=last),
                            deps=kdeps if cc == 0 else [], inc=(cc == CC - 1))
                        mmv = op("tensor", lambda e, cc=cc, first=(cc == 0),
                                 last=(cc == CC - 1): e.matmul(
                            pb[3][:, 0:KV],
                            lnscr[:, cc * 128:(cc + 1) * 128],
                            wvb[:, cc * KV:(cc + 1) * KV],
                            start=first, stop=last),
                            deps=[], inc=(cc == CC - 1))
                    tve = op("vector", lambda e, g=g: e.tensor_copy(
                        vt[:, g * KV:(g + 1) * KV], pb[3][:, 0:KV]), deps=[mmv])
                    set_bank_dep(3, [tve])
                    qs = []
                    for gg in range(G):
                        qs.append(op("scalar", lambda e, gg=gg: e.activation(
                            stg[:, gg * 128:(gg + 1) * 128],
                            pb[2][:, gg * 128:(gg + 1) * 128], ACT.Square,
                            accum_out=stats[:, 11 + gg:12 + gg]), deps=[mmk]))
                    trms = op("scalar", lambda e: e.activation(
                        stats[:, 11:11 + G], stats[:, 11:11 + G], ACT.Sqrt,
                        scale=1.0 / 128.0), deps=qs)
                    tfac = op("vector", lambda e: e.tensor_scalar_add(
                        stats[:, 11:11 + G], stats[:, 11:11 + G], 1e-6), deps=[trms])
                    tfac = op("vector", lambda e: e.reciprocal(
                        stats[:, 11:11 + G], stats[:, 11:11 + G]), deps=[tfac])
                    kapp = []
                    for gg in range(G):
                        kapp.append(op("vector", lambda e, gg=gg:
                                       e.scalar_tensor_tensor(
                            knsb[:, gg * 128:(gg + 1) * 128],
                            pb[2][:, gg * 128:(gg + 1) * 128],
                            stats[:, 11 + gg:12 + gg], knwr[:],
                            op0=ALU.mult, op1=ALU.mult), deps=[tfac, mmk]))
                    set_bank_dep(2, kapp)
                    slk = (g % 2) * KV
                    kn3 = knsb[:].rearrange("p (g z) -> p g z", g=G)
                    kr3 = krb[:, slk:slk + KV].rearrange(
                        "p (g z) -> p g z", g=G)
                    cb = cosgb[:, g * 64:(g + 1) * 64].unsqueeze(1) \
                        .broadcast_to([128, G, 64])
                    sbr = singb[:, g * 64:(g + 1) * 64].unsqueeze(1) \
                        .broadcast_to([128, G, 64])
                    r1 = rt1[:, 0:G * 64].rearrange("p (g z) -> p g z", g=G)
                    r2 = rt2[:, 0:G * 64].rearrange("p (g z) -> p g z", g=G)
                    t1 = op("vector", lambda e, kn3=kn3, cb=cb, r1=r1:
                            e.tensor_tensor(r1, kn3[:, :, 0:64], cb,
                                            op=ALU.mult), deps=kapp)
                    t2 = op("vector", lambda e, kn3=kn3, sbr=sbr, r2=r2:
                            e.tensor_tensor(r2, kn3[:, :, 64:128], sbr,
                                            op=ALU.mult), deps=kapp)
                    t3 = op("vector", lambda e, kr3=kr3, r1=r1, r2=r2:
                            e.tensor_tensor(kr3[:, :, 0:64], r1, r2,
                                            op=ALU.add),
                            deps=[t1, t2] + cur.get(f"krb_free{slk}", []))
                    t4 = op("vector", lambda e, kn3=kn3, cb=cb, r1=r1:
                            e.tensor_tensor(r1, kn3[:, :, 64:128], cb,
                                            op=ALU.mult), deps=[t3])
                    t5 = op("vector", lambda e, kn3=kn3, sbr=sbr, r2=r2:
                            e.tensor_tensor(r2, kn3[:, :, 0:64], sbr,
                                            op=ALU.mult), deps=[t3])
                    t6 = op("vector", lambda e, kr3=kr3, r1=r1, r2=r2:
                            e.tensor_tensor(kr3[:, :, 64:128], r1, r2,
                                            op=ALU.subtract), deps=[t4, t5])
                    if kpend[0] is not None:
                        emit_ktr(kpend[0])
                    kpend[0] = (g, t3, t6, slk)
                emit_ktr(kpend[0])
                kpend[0] = None

                mm = None
                for kc in range(NT):
                    mm = op("tensor", lambda e, kc=kc, first=(kc == 0),
                            last=(kc == NT - 1): e.matmul(
                        pb[5][0:NV, 0:KV],
                        smb[:, kc * NV:(kc + 1) * NV],
                        vt[:, kc * KV:(kc + 1) * KV],
                        start=first, stop=last),
                        deps=(bank_dep(5) + [(sv["vector"], sv["vector"].n)])
                        if kc == 0 else [], inc=(kc == NT - 1))
                tvs = op("vector", lambda e: e.tensor_copy(
                    vsuf[0:NV - 1, :], pb[5][0:NV - 1, 0:KV]), deps=[mm])
                set_bank_dep(5, [tvs])
                for gg in range(G):
                    bi = 6 + (gg % 2)
                    mt = op("tensor", lambda e, gg=gg, bi=bi: e.matmul(
                        pb[bi][:, 0:NV],
                        vsuf[0:NV, gg * 128:(gg + 1) * 128],
                        idb[0:NV, 0:NV], start=True, stop=True),
                        deps=[tvs, (DS("res"), DS("res").n)] + bank_dep(bi))
                    tv2 = op("vector", lambda e, gg=gg, bi=bi:
                             e.tensor_scalar_mul(
                        vsufT[:, gg * NV:(gg + 1) * NV], pb[bi][:, 0:NV],
                        EXPM50), deps=[mt])
                    set_bank_dep(bi, [tv2])

                run_phase(dma_waits=[(DS("x0"), DS("x0").n),
                                     (DS("x1"), DS("x1").n)])

            lnTo = sb(sAB, "lnTo", [128, NJ * C], BF16)

            sA1b = ExitStack()
            with sA1b:
                xb2 = [sb(sA1b, f"xb2_{i}", [128, C], F32) for i in range(2)]
                lnbB = sb(sA1b, "lnbB", [128, C], BF16)
                statsB = sb(sA1b, "statsB", [128, 16], F32)
                stgB = sb(sA1b, "stgB", [128, max(C, 512)], F32)

                phase("a1b")
                cur.pop("lnb_free", None)
                cur.pop("xfree0", None)
                cur.pop("xfree1", None)
                for j in range(NJ):
                    buf = xb2[j % 2]
                    ld = dma("sync" if j % 2 == 0 else "gpsimd",
                             buf[:], xo[j * 128:(j + 1) * 128, :],
                             f"x{j % 2}", deps=cur.get(f"xfree{j % 2}", []))
                    tln = emit_ln(buf, ld, lnbB, statsB, stgB)
                    cur[f"xfree{j % 2}"] = [tln]
                    emit_transposes(
                        lambda gr, j=j: lnTo[:, j * C + gr * 512:
                                             j * C + (gr + 1) * 512],
                        tln, 0, 1, lnbB)
                run_phase(dma_waits=[(DS("x0"), DS("x0").n),
                                     (DS("x1"), DS("x1").n)])

            sB = ExitStack()
            with sB:
                wqb = [sb(sB, f"wqb{i}", [128, CC * 512], BF16) for i in range(2)]
                qfb = sb(sB, "qfb", [128, NJ * 512], BF16)
                tsbT = sb(sB, "tsbT", [128, 4 * 512], F32)
                esbT = sb(sB, "esbT", [128, 4 * 512], BF16)
                qnsb = sb(sB, "qnsb", [128, 512], F32)
                qrb = sb(sB, "qrb", [128, 2 * 512], BF16)
                brt1 = sb(sB, "brt1", [128, 256], F32)
                brt2 = sb(sB, "brt2", [128, 256], F32)
                ysb = sb(sB, "ysb", [128, 512], F32)
                ytb = sb(sB, "ytb", [128, 512], BF16)
                bst = sb(sB, "bst", [128, 16], F32)
                zbb = sb(sB, "zbb", [128, 512], F32)
                SBANKS = (0, 1, 4)

                phase("b")
                wq_free = {0: [], 1: []}
                qf_free = []
                pend_fin = [None]

                def emit_finalize(u):
                    j, g, ztok = u["j"], u["g"], u["ztok"]
                    cnt_add = float((T - 128 * (2 * j + 2)) * EXPM50)
                    vcol = vsufT[:, g * NV + (2 * j + 2):
                                 g * NV + (2 * j + 3)].broadcast_to([128, 512])
                    v3 = op("vector", lambda e, vcol=vcol: e.tensor_tensor(
                        ysb[:], pb[2][:, 0:512], vcol, op=ALU.add),
                        deps=[ztok] + cur.get("ysb_free", []))
                    set_bank_dep(2, [v3])
                    v1 = op("vector", lambda e, cnt_add=cnt_add:
                            e.tensor_scalar_add(
                        zbb[:], pb[3][:, 0:512], cnt_add),
                        deps=[ztok])
                    set_bank_dep(3, [v1])
                    v2 = op("vector", lambda e: e.reciprocal(
                        zbb[:], zbb[:]), deps=[v1])
                    v4 = op("vector", lambda e: e.tensor_mul(
                        ytb[:], ysb[:], zbb[:]),
                        deps=[v3, v2] + cur.get("ytb_free", []))
                    cur["ysb_free"] = [v4]
                    dma("sync", yt[g * 512:(g + 1) * 512,
                                   j * 128:(j + 1) * 128]
                        .rearrange("(h p) q -> p h q", h=4),
                        ytb[:].rearrange("p (h q) -> p h q", h=4),
                        "yst", deps=[v4])
                    cur["ytb_free"] = [(DS("yst"), DS("yst").n)]

                for g in range(G):
                    wbuf = wqb[g % 2]
                    lw = dma("gpsimd",
                             wbuf[:].rearrange("p (cc m) -> p cc m", cc=CC),
                             wq[g].rearrange("(cc p) m -> p cc m", p=128),
                             f"wq{g % 2}", deps=wq_free[g % 2])
                    qpend = [None]

                    def emit_qtr(u):
                        j2, t3q, t6q, sl = u
                        qtt = []
                        for k in range(4):
                            qtt.append(op("tensor", lambda e, k=k, sl=sl:
                                          e.transpose(
                                pbb[5][:, k * 128:(k + 1) * 128],
                                qrb[:, sl + k * 128:sl + (k + 1) * 128],
                                idb[:]),
                                deps=([t3q, t6q] + bank_dep(5))
                                if k == 0 else []))
                        cur[f"qrb_free{sl}"] = [qtt[-1]] + qtt[:-1]
                        tqe = op("vector", lambda e, j2=j2: e.tensor_copy(
                            qfb[:, j2 * 512:(j2 + 1) * 512], pbb[5][:, 0:512]),
                            deps=[qtt[-1]] + qtt[:-1] + qf_free)
                        set_bank_dep(5, [tqe])
                        cur[f"qf_last{j2}"] = tqe

                    for j in range(NJ):
                        bq = pb[6 + (j % 2)]
                        mmq = None
                        dep0 = [lw] + bank_dep(6 + (j % 2))
                        for cc in range(CC):
                            mmq = op("tensor", lambda e, j=j, cc=cc, bq=bq,
                                     wbuf=wbuf,
                                     first=(cc == 0), last=(cc == CC - 1):
                                     e.matmul(
                                bq[:, 0:512],
                                lnTo[:, j * C + cc * 128:j * C + (cc + 1) * 128],
                                wbuf[:, cc * 512:(cc + 1) * 512],
                                start=first, stop=last),
                                deps=dep0 if cc == 0 else [],
                                inc=(cc == CC - 1))
                        qb0 = (j % 2) * 8
                        qs = []
                        for k in range(4):
                            qs.append(op("scalar", lambda e, k=k, bq=bq,
                                         qb0=qb0: e.activation(
                                tsbT[:, k * 128:(k + 1) * 128],
                                bq[:, k * 128:(k + 1) * 128],
                                ACT.Square,
                                accum_out=bst[:, qb0 + k:qb0 + k + 1]),
                                deps=[mmq]))
                        tq = op("scalar", lambda e, qb0=qb0: e.activation(
                            bst[:, qb0:qb0 + 4], bst[:, qb0:qb0 + 4], ACT.Sqrt,
                            scale=1.0 / 128.0), deps=qs)
                        tq = op("vector", lambda e, qb0=qb0: e.tensor_scalar_add(
                            bst[:, qb0:qb0 + 4], bst[:, qb0:qb0 + 4], 1e-6),
                            deps=[tq])
                        tq = op("vector", lambda e, qb0=qb0: e.reciprocal(
                            bst[:, qb0:qb0 + 4], bst[:, qb0:qb0 + 4]),
                            deps=[tq])
                        qapp = []
                        for k in range(4):
                            qapp.append(op("vector", lambda e, k=k, bq=bq,
                                           qb0=qb0: e.scalar_tensor_tensor(
                                qnsb[:, k * 128:(k + 1) * 128],
                                bq[:, k * 128:(k + 1) * 128],
                                bst[:, qb0 + k:qb0 + k + 1], qnwr[:],
                                op0=ALU.mult, op1=ALU.mult),
                                deps=[tq, mmq]))
                        set_bank_dep(6 + (j % 2), qapp)
                        sl = (j % 2) * 512
                        qn3 = qnsb[:].rearrange("p (h z) -> p h z", h=4)
                        qr3 = qrb[:, sl:sl + 512].rearrange(
                            "p (h z) -> p h z", h=4)
                        cbo = cosob[:, j * 64:(j + 1) * 64].unsqueeze(1) \
                            .broadcast_to([128, 4, 64])
                        sbo = sinob[:, j * 64:(j + 1) * 64].unsqueeze(1) \
                            .broadcast_to([128, 4, 64])
                        r1 = brt1[:].rearrange("p (h z) -> p h z", h=4)
                        r2 = brt2[:].rearrange("p (h z) -> p h z", h=4)
                        t1 = op("vector", lambda e, qn3=qn3, cbo=cbo, r1=r1:
                                e.tensor_tensor(r1, qn3[:, :, 0:64], cbo,
                                                op=ALU.mult), deps=qapp)
                        t2 = op("vector", lambda e, qn3=qn3, sbo=sbo, r2=r2:
                                e.tensor_tensor(r2, qn3[:, :, 64:128], sbo,
                                                op=ALU.mult), deps=qapp)
                        t3 = op("vector", lambda e, qr3=qr3, r1=r1, r2=r2:
                                e.tensor_tensor(qr3[:, :, 0:64], r1, r2,
                                                op=ALU.add),
                                deps=[t1, t2] + cur.get(f"qrb_free{sl}", []))
                        t4 = op("vector", lambda e, qn3=qn3, cbo=cbo, r1=r1:
                                e.tensor_tensor(r1, qn3[:, :, 64:128], cbo,
                                                op=ALU.mult), deps=[t3])
                        t5 = op("vector", lambda e, qn3=qn3, sbo=sbo, r2=r2:
                                e.tensor_tensor(r2, qn3[:, :, 0:64], sbo,
                                                op=ALU.mult), deps=[t3])
                        t6 = op("vector", lambda e, qr3=qr3, r1=r1, r2=r2:
                                e.tensor_tensor(qr3[:, :, 64:128], r1, r2,
                                                op=ALU.subtract), deps=[t4, t5])
                        if qpend[0] is not None:
                            emit_qtr(qpend[0])
                        qpend[0] = (j, t3, t6, sl)
                    emit_qtr(qpend[0])
                    qpend[0] = None
                    wq_free[g % 2] = [(sv["tensor"], sv["tensor"].n)]

                    for j in range(NJ):
                        nk = 2 * j + 2
                        qdep = [cur[f"qf_last{j}"]]
                        stoks = {}
                        ttoks = {}
                        mtoks = {}
                        etoks = {}
                        ztoks = {}

                        def do_S(kc):
                            bi = SBANKS[kc % 3]
                            stoks[kc] = op("tensor", lambda e, kc=kc, bi=bi,
                                           j=j, g=g: e.matmul(
                                pb[bi][:, 0:512],
                                kf[:, g * T + kc * 128:g * T + (kc + 1) * 128],
                                qfb[:, j * 512:(j + 1) * 512],
                                start=True, stop=True),
                                deps=qdep + bank_dep(bi))

                        def do_T(kc):
                            bi = SBANKS[kc % 3]
                            sl = tsbT[:, (kc % 4) * 512:(kc % 4) * 512 + 512]
                            tt = op("scalar", lambda e, bi=bi, sl=sl:
                                    e.activation(
                                sl, pb[bi][:, 0:512], ACT.Tanh,
                                scale=SCALE_IN),
                                deps=[stoks[kc]])
                            set_bank_dep(bi, [tt])
                            ttoks[kc] = tt
                            if kc >= nk - 2:
                                tmt = (kc - (nk - 2))
                                sl3 = sl.rearrange("p (h q) -> p h q", h=4)
                                cm3 = cmsk[:, tmt * 128:(tmt + 1) * 128] \
                                    .unsqueeze(1).broadcast_to([128, 4, 128])
                                mtoks[kc] = op("vector", lambda e, sl3=sl3,
                                               cm3=cm3: e.tensor_tensor(
                                    sl3, sl3, cm3, op=ALU.min), deps=[tt])

                        def do_E(pp):
                            base = (pp % 2) * 1024
                            sbase = (2 * pp % 4) * 512
                            deps = [ttoks[2 * pp], ttoks[2 * pp + 1]]
                            if 2 * pp in mtoks:
                                deps.append(mtoks[2 * pp])
                            if 2 * pp + 1 in mtoks:
                                deps.append(mtoks[2 * pp + 1])
                            deps += cur.get(f"esb_free{pp % 2}", [])
                            etoks[pp] = op("scalar", lambda e, base=base,
                                           sbase=sbase: e.activation(
                                esbT[:, base:base + 1024],
                                tsbT[:, sbase:sbase + 1024],
                                ACT.Exp, scale=50.0), deps=deps)

                        def do_PVZ(kc):
                            pp = kc // 2
                            off = (pp % 2) * 1024 + (kc % 2) * 512
                            op("tensor", lambda e, kc=kc, off=off, g=g,
                               first=(kc == 0), last=(kc == nk - 1): e.matmul(
                                pb[2][:, 0:512],
                                vt[:, kc * KV + g * 128:kc * KV + (g + 1) * 128],
                                esbT[:, off:off + 512],
                                start=first, stop=last),
                                deps=[etoks[pp]] +
                                (bank_dep(2) if kc == 0 else []), inc=False)
                            ztoks[kc] = op("tensor", lambda e, kc=kc, off=off,
                                           first=(kc == 0),
                                           last=(kc == nk - 1): e.matmul(
                                pb[3][:, 0:512], onesm[:],
                                esbT[:, off:off + 512],
                                start=first, stop=last),
                                deps=(bank_dep(3) if kc == 0 else []))
                            if kc % 2 == 1:
                                cur[f"esb_free{pp % 2}"] = [ztoks[kc]]

                        do_S(0)
                        do_T(0)
                        do_S(1)
                        do_T(1)
                        do_E(0)
                        if pend_fin[0] is not None:
                            emit_finalize(pend_fin[0])
                        for pp in range(nk // 2):
                            for kc2 in (2 * pp + 2, 2 * pp + 3):
                                if kc2 < nk:
                                    do_S(kc2)
                                    do_T(kc2)
                            if pp + 1 < nk // 2:
                                do_E(pp + 1)
                            do_PVZ(2 * pp)
                            do_PVZ(2 * pp + 1)
                        pend_fin[0] = dict(j=j, g=g, ztok=ztoks[nk - 1])
                    qf_free = [(sv["tensor"], sv["tensor"].n)]
                if pend_fin[0] is not None:
                    emit_finalize(pend_fin[0])
                    pend_fin[0] = None
                run_phase(dma_waits=[(DS("yst"), DS("yst").n),
                                     (DS("wq0"), DS("wq0").n),
                                     (DS("wq1"), DS("wq1").n)])

        sB2 = ExitStack()
        with sB2:
            TBT = min(4, NJ)
            TBW = TBT * 128
            wob = sb(sB2, "wob", [128, CC * C], BF16)
            ytl = [sb(sB2, f"ytl{i}", [128, CC * TBW], BF16) for i in range(2)]
            afb = sb(sB2, "afb", [128, CC * TBW], F32)
            xob = [sb(sB2, f"xob{i}", [128, C], F32) for i in range(2)]
            bor = sb(sB2, "bor", [128, C], F32)
            xbo = sb(sB2, "xbo", [128, C], F32)
            x1s = [sb(sB2, f"x1s{i}", [128, C], F32) for i in range(2)]
            idb2 = sb(sB2, "idb2", [128, 128], BF16)
            idf2 = sb(sB2, "idf2", [128, 128], F32)

            bank_deps = {i: [] for i in range(8)}

            def bank_dep(i):
                return bank_deps[i]

            def set_bank_dep(i, toks):
                bank_deps[i] = list(toks)

            phase("b2")
            lwo = dma("gpsimd", wob[:].rearrange("p (jj c) -> p jj c", jj=CC),
                      wo.rearrange("(jj p) c -> p jj c", p=128), "wo")
            dma("sync", bor[:], bo[None, :].broadcast_to([128, C]), "bo")
            dma("sync", idb2[:], idb_in, "bo")
            dma("sync", idf2[:], idf_in, "bo")
            ytw = (DS("yst"), DS("yst").n)
            bodep = (DS("bo"), DS("bo").n)
            ytl_free = {0: [], 1: []}
            x1_free = {0: [], 1: []}
            xo_free = {0: [], 1: []}
            for tb in range(NJ // TBT):
                buf = ytl[tb % 2]
                ldy = dma("scalar", buf[:].rearrange("p (jj t) -> p jj t", jj=CC),
                          yt[:, tb * TBW:(tb + 1) * TBW].rearrange(
                              "(jj p) t -> p jj t", p=128),
                          f"ytl{tb % 2}", deps=[ytw] + ytl_free[tb % 2])
                afdone = []
                for cb in range(CC):
                    bk = pb[cb % 2]
                    mmo = None
                    for jj in range(CC):
                        mmo = op("tensor", lambda e, cb=cb, jj=jj, bk=bk,
                                 buf=buf, first=(jj == 0),
                                 last=(jj == CC - 1): e.matmul(
                            bk[:, 0:TBW],
                            wob[:, jj * C + cb * 128:jj * C + (cb + 1) * 128],
                            buf[:, jj * TBW:(jj + 1) * TBW],
                            start=first, stop=last),
                            deps=([ldy, lwo] + bank_dep(cb % 2))
                            if jj == 0 else [], inc=(jj == CC - 1))
                    tev = op("vector", lambda e, cb=cb, bk=bk: e.tensor_copy(
                        afb[:, cb * TBW:(cb + 1) * TBW], bk[:, 0:TBW]),
                        deps=[mmo])
                    set_bank_dep(cb % 2, [tev])
                    afdone.append(tev)
                ytl_free[tb % 2] = [afdone[-1]]
                for q in range(TBT):
                    tt = tb * TBT + q
                    ldx = dma("gpsimd", xob[tt % 2][:],
                              xo[tt * 128:(tt + 1) * 128, :],
                              f"xo{tt % 2}", deps=xo_free[tt % 2])
                    txb = op("vector", lambda e, tt=tt: e.tensor_add(
                        xbo[:], xob[tt % 2][:], bor[:]),
                        deps=[ldx, bodep] + cur.get("xbo_free", []))
                    trs = []
                    for cb in range(CC):
                        bk = pb[4 + ((cb // 4) % 2)]
                        ttr = op("tensor", lambda e, cb=cb, q=q, bk=bk:
                                 e.transpose(
                            bk[:, (cb % 4) * 128:(cb % 4 + 1) * 128],
                            afb[:, cb * TBW + q * 128:cb * TBW + (q + 1) * 128],
                            idf2[:]),
                            deps=(afdone + bank_dep(4 + ((cb // 4) % 2)) +
                                  [bodep])
                            if cb % 4 == 0 else [],
                            inc=(cb % 4 == 3))
                        if cb % 4 == 3:
                            tadd = op("vector", lambda e, cb=cb, tt=tt, bk=bk:
                                      e.tensor_add(
                                x1s[tt % 2][:, (cb - 3) * 128:(cb + 1) * 128],
                                bk[:, 0:512],
                                xbo[:, (cb - 3) * 128:(cb + 1) * 128]),
                                deps=[ttr, txb] + x1_free[tt % 2])
                            set_bank_dep(4 + ((cb // 4) % 2), [tadd])
                            trs.append(tadd)
                    cur["xbo_free"] = [trs[-1]]
                    std = dma("sync", out[tt * 128:(tt + 1) * 128, :],
                              x1s[tt % 2][:], f"x1st{tt % 2}", deps=trs)
                    x1_free[tt % 2] = [std]
                    xo_free[tt % 2] = [txb]
            run_phase(dma_waits=[(DS("x1st0"), DS("x1st0").n),
                                 (DS("x1st1"), DS("x1st1").n),
                                 (DS("wo"), DS("wo").n),
                                 (DS("ytl0"), DS("ytl0").n),
                                 (DS("ytl1"), DS("ytl1").n)])

        sF = ExitStack()
        with sF:
            hb = sb(sF, "hb", [128, NF * BT], BF16)
            adb = sb(sF, "adb", [128, NF * 72], BF16)
            bdb = sb(sF, "bdb", [72, C], BF16)
            rad = sb(sF, "rad", [72, BT], BF16)
            agub = sb(sF, "agub", [128, CC * 152], BF16)
            bgb = sb(sF, "bgb", [72, DFF], BF16)
            bub = sb(sF, "bub", [72, DFF], BF16)
            wsc = sb(sF, "wsc", [128, NTT * 8], F32)
            ebr = sb(sF, "ebr", [128, 8], F32)
            idf3 = sb(sF, "idf3", [128, 128], F32)
            idb3 = sb(sF, "idb3", [128, 128], BF16)

            first_res = [True]
            for blk in range(NB):
                sF1 = ExitStack()
                with sF1:
                    ln2f = sb(sF1, f"ln2f_{blk}", [128, CC * BT], BF16)
                    wgub = [sb(sF1, f"wgub{i}_{blk}", [128, 2 * CC * 128], BF16)
                            for i in range(2)]
                    x1l = [sb(sF1, f"x1l{i}_{blk}", [128, C], F32) for i in range(2)]
                    lnb2 = sb(sF1, f"lnb2_{blk}", [128, C], BF16)
                    sg = [sb(sF1, f"sg{i}_{blk}", [128, NSP], F32) for i in range(2)]
                    raf = sb(sF1, f"raf_{blk}", [128, 72], F32)
                    ragf = sb(sF1, f"ragf_{blk}", [72, BT], BF16)
                    rauf = sb(sF1, f"rauf_{blk}", [72, BT], BF16)
                    fst = sb(sF1, f"fst_{blk}", [128, 24], F32)
                    scrw = sb(sF1, f"scrw_{blk}", [128, 48], F32)

                    bank_deps = {i: [] for i in range(8)}
                    phase(f"f1_{blk}")
                    if first_res[0]:
                        dma("gpsimd", agub[:].rearrange(
                            "p (cc r) -> p cc r", cc=CC),
                            agu.rearrange("(cc p) r -> p cc r", p=128), "fres")
                        dma("gpsimd", bgb[:], bg, "fres")
                        dma("gpsimd", bub[:], bu, "fres")
                        dma("gpsimd", adb[:].rearrange(
                            "p (fc r) -> p fc r", fc=NF),
                            ad.rearrange("(fc p) r -> p fc r", p=128), "fres")
                        dma("gpsimd", bdb[:], bd, "fres")
                        dma("sync", ebr[:],
                            ebias[None, :].broadcast_to([128, 8]), "fres2")
                        dma("sync", idf3[:], idf_in, "fres2")
                        dma("sync", idb3[:], idb_in, "fres2")
                        first_res[0] = False
                    fdep = [(DS("fres"), DS("fres").n),
                            (DS("fres2"), DS("fres2").n)]

                    rapend = [None]

                    def emit_ra(u):
                        tt2, bc2, mr2, tws2, tsg2 = u
                        ra_toks = []
                        for (cb0, dst) in ((0, ragf), (72, rauf)):
                            t1 = op("vector", lambda e, cb0=cb0, tt2=tt2,
                                    bc2=bc2: e.tensor_tensor(
                                raf[:, 0:64].rearrange("p (x r) -> p x r", x=8),
                                pb[bc2][:, cb0:cb0 + 64].rearrange(
                                    "p (x r) -> p x r", x=8),
                                wsc[:, tt2 * 8:(tt2 + 1) * 8].unsqueeze(2)
                                .broadcast_to([128, 8, 8]),
                                op=ALU.mult), deps=[mr2, tws2])
                            t2 = op("vector", lambda e, cb0=cb0, bc2=bc2:
                                    e.tensor_scalar_mul(
                                raf[:, 64:72], pb[bc2][:, cb0 + 64:cb0 + 72],
                                0.125), deps=[t1])
                            ra_toks += [t1, t2]
                            ttr = op("tensor", lambda e: e.transpose(
                                pb[5][0:72, 0:128], raf[:, 0:72], idf3[:]),
                                deps=[t1, t2, fdep[1]] + bank_dep(5))
                            tre = op("vector", lambda e, dst=dst, tt2=tt2:
                                     e.tensor_copy(
                                dst[0:72, tt2 * 128:(tt2 + 1) * 128],
                                pb[5][0:72, 0:128]), deps=[ttr])
                            set_bank_dep(5, [tre])
                            cur[f"ra_{id(dst)}_{tt2}"] = tre
                        set_bank_dep(bc2, ra_toks + [tsg2])

                    x1w = [(DS("x1st0"), DS("x1st0").n), (DS("x1st1"), DS("x1st1").n)]
                    x1_free2 = {0: [], 1: []}
                    ln_done = []
                    for tt in range(NTT):
                        trow = blk * BT + tt * 128
                        buf = x1l[tt % 2]
                        ld = dma("sync" if tt % 2 == 0 else "scalar",
                                 buf[:], out[trow:trow + 128, :],
                                 f"x1l{tt % 2}", deps=x1w + x1_free2[tt % 2])
                        lnb2_free = cur.get("lnb2_free", [])
                        NSS2 = C // 512
                        jview = ln2f[:].rearrange("p (cc t) -> p cc t", cc=CC)
                        sqs = []
                        for i in range(NSS2):
                            sqs.append(op("scalar", lambda e, buf=buf, i=i,
                                          tt=tt, jview=jview: e.activation(
                                jview[:, 4 * i:4 * i + 4,
                                      tt * 128:(tt + 1) * 128],
                                buf[:, i * 512:(i + 1) * 512],
                                ACT.Square, accum_out=fst[:, i:i + 1]),
                                deps=[ld] + (lnb2_free if i == 0 else [])))
                        tsum = op("vector", lambda e, buf=buf: e.reduce_sum(
                            fst[:, 4:5], buf[:], axis=AX.X), deps=[ld])
                        tmu = op("vector", lambda e: e.tensor_scalar_mul(
                            fst[:, 5:6], fst[:, 4:5], -1.0 / C), deps=[tsum])
                        tm2 = op("vector", lambda e: e.tensor_mul(
                            fst[:, 6:7], fst[:, 5:6], fst[:, 5:6]), deps=[tmu])
                        tex = op("vector", lambda e: e.reduce_sum(
                            fst[:, 7:8], fst[:, 0:NSS2], axis=AX.X), deps=sqs)
                        tvar = op("vector", lambda e: e.tensor_scalar(
                            fst[:, 8:9], fst[:, 7:8], 1.0 / C, fst[:, 6:7],
                            op0=ALU.mult, op1=ALU.subtract), deps=[tex, tm2])
                        tsd = op("scalar", lambda e: e.activation(
                            fst[:, 9:10], fst[:, 8:9], ACT.Sqrt,
                            bias=cvr[:, 0:1]), deps=[tvar])
                        trs = op("vector", lambda e: e.reciprocal(
                            fst[:, 10:11], fst[:, 9:10]), deps=[tsd])
                        tln = op("vector", lambda e, buf=buf: e.tensor_scalar(
                            lnb2[:], buf[:], fst[:, 5:6], fst[:, 10:11],
                            op0=ALU.add, op1=ALU.mult),
                            deps=[tmu, trs] + sqs + lnb2_free)
                        x1_free2[tt % 2] = [tln]
                        for grp in range(CC // 4):
                            bi = grp % 2
                            tts = []
                            for q in range(4):
                                cc = grp * 4 + q
                                tts.append(op("tensor",
                                              lambda e, bi=bi, q=q, cc=cc:
                                              e.transpose(
                                    pbb[bi][:, q * 128:(q + 1) * 128],
                                    lnb2[:, cc * 128:(cc + 1) * 128],
                                    idb3[:]),
                                    deps=([tln, fdep[1]] + bank_dep(grp % 2))
                                    if q == 0 else [], inc=(q == 3)))
                            dst = ln2f[:].rearrange(
                                "p (cc t) -> p cc t", cc=CC)
                            tev = op("vector", lambda e, grp=grp, tt=tt,
                                     bi=bi, dst=dst: e.tensor_copy(
                                dst[:, grp * 4:(grp + 1) * 4,
                                    tt * 128:(tt + 1) * 128],
                                pbb[bi][:, 0:512].rearrange(
                                    "p (a z) -> p a z", a=4)),
                                deps=[tts[-1]])
                            bank_deps[grp % 2] = [tev]
                            ln_done.append(tev)
                            cur["lnb2_free"] = [tts[-1]]
                        bc = 6 + (tt % 2)
                        mr = None
                        for cc in range(CC):
                            mr = op("tensor", lambda e, cc=cc, tt=tt, bc=bc,
                                    first=(cc == 0), last=(cc == CC - 1):
                                    e.matmul(
                                pb[bc][:, 0:152],
                                ln2f[:].rearrange("p (cc t) -> p cc t", cc=CC)
                                [:, cc, tt * 128:(tt + 1) * 128],
                                agub[:, cc * 152:(cc + 1) * 152],
                                start=first, stop=last),
                                deps=(ln_done[-4:] + fdep + bank_dep(bc))
                                if cc == 0 else [], inc=(cc == CC - 1))
                        if rapend[0] is not None:
                            emit_ra(rapend[0])
                        tsg = op("scalar", lambda e, bc=bc: e.activation(
                            scrw[:, 0:8], pb[bc][:, 144:152], ACT.Sigmoid),
                            deps=[mr] + cur.get("scrw_free", []))
                        tb_ = op("vector", lambda e: e.tensor_add(
                            scrw[:, 8:16], scrw[:, 0:8], ebr[:]),
                            deps=[tsg, fdep[1]])
                        tm1 = op("vector", lambda e: e.reduce_max(
                            fst[:, 16:17], scrw[:, 8:16], axis=AX.X),
                            deps=[tb_])
                        tk1 = op("vector", lambda e: e.tensor_scalar(
                            scrw[:, 16:24], scrw[:, 8:16], fst[:, 16:17], None,
                            op0=ALU.is_ge), deps=[tm1])
                        ts1 = op("vector", lambda e: e.tensor_mul(
                            scrw[:, 24:32], scrw[:, 16:24], scrw[:, 0:8]),
                            deps=[tk1])
                        ts1r = op("vector", lambda e: e.reduce_sum(
                            fst[:, 17:18], scrw[:, 24:32], axis=AX.X),
                            deps=[ts1])
                        tb2 = op("vector", lambda e: e.scalar_tensor_tensor(
                            scrw[:, 32:40], scrw[:, 16:24], -1e9,
                            scrw[:, 8:16], op0=ALU.mult, op1=ALU.add),
                            deps=[tk1])
                        tm2_ = op("vector", lambda e: e.reduce_max(
                            fst[:, 18:19], scrw[:, 32:40], axis=AX.X),
                            deps=[tb2])
                        tk2 = op("vector", lambda e: e.tensor_scalar(
                            scrw[:, 32:40], scrw[:, 32:40], fst[:, 18:19],
                            None, op0=ALU.is_ge), deps=[tm2_])
                        ts2 = op("vector", lambda e: e.tensor_mul(
                            scrw[:, 40:48], scrw[:, 32:40], scrw[:, 0:8]),
                            deps=[tk2])
                        cur["scrw_free"] = [ts2]
                        ts2r = op("vector", lambda e: e.reduce_sum(
                            fst[:, 19:20], scrw[:, 40:48], axis=AX.X),
                            deps=[ts2])
                        tden = op("vector", lambda e: e.tensor_add(
                            fst[:, 20:21], fst[:, 17:18], fst[:, 19:20]),
                            deps=[ts1r, ts2r])
                        tden = op("vector", lambda e: e.tensor_scalar_add(
                            fst[:, 20:21], fst[:, 20:21], 1e-8), deps=[tden])
                        tinv = op("vector", lambda e: e.reciprocal(
                            fst[:, 21:22], fst[:, 20:21]), deps=[tden])
                        tw1 = op("vector", lambda e: e.tensor_scalar(
                            scrw[:, 16:24], scrw[:, 16:24], fst[:, 17:18],
                            None, op0=ALU.mult), deps=[tk1, ts1r])
                        tw2 = op("vector", lambda e: e.tensor_scalar(
                            scrw[:, 32:40], scrw[:, 32:40], fst[:, 19:20],
                            None, op0=ALU.mult), deps=[tk2, ts2r])
                        twa = op("vector", lambda e: e.tensor_add(
                            scrw[:, 16:24], scrw[:, 16:24], scrw[:, 32:40]),
                            deps=[tw1, tw2])
                        tws = op("vector", lambda e, tt=tt: e.tensor_scalar(
                            wsc[:, tt * 8:(tt + 1) * 8], scrw[:, 16:24],
                            fst[:, 21:22], 0.125, op0=ALU.mult, op1=ALU.mult),
                            deps=[twa, tinv])
                        rapend[0] = (tt, bc, mr, tws, tsg)

                    if rapend[0] is not None:
                        emit_ra(rapend[0])
                        rapend[0] = None
                    ra_done = [(sv["vector"], sv["vector"].n)]
                    wgu_free = {0: [], 1: []}
                    for m in range(NF):
                        wb_ = wgub[m % 2]
                        lw = dma("gpsimd",
                                 wb_[:].rearrange(
                                     "p (a cc z) -> p a cc z", a=2, cc=CC),
                                 wgu[m].rearrange(
                                     "a (cc p) z -> p a cc z", p=128),
                                 f"wgu{m % 2}", deps=wgu_free[m % 2])
                        hdone = []
                        for sp in range(NSB):
                            mgs = mus = None
                            for cc in range(CC):
                                mgs = op("tensor", lambda e, cc=cc, sp=sp,
                                         wb_=wb_, first=(cc == 0):
                                         e.matmul(
                                    pb[0 + sp][:, 0:NSP],
                                    wb_[:, (0 * CC + cc) * 128:
                                        (0 * CC + cc + 1) * 128],
                                    ln2f[:].rearrange(
                                        "p (cc t) -> p cc t", cc=CC)
                                    [:, cc, sp * NSP:(sp + 1) * NSP],
                                    start=first, stop=False),
                                    deps=([lw] + ln_done[-1:] +
                                          bank_dep(0 + sp) + ra_done)
                                    if cc == 0 else [], inc=False)
                                mus = op("tensor", lambda e, cc=cc, sp=sp,
                                         wb_=wb_, first=(cc == 0):
                                         e.matmul(
                                    pb[2 + sp][:, 0:NSP],
                                    wb_[:, (1 * CC + cc) * 128:
                                        (1 * CC + cc + 1) * 128],
                                    ln2f[:].rearrange(
                                        "p (cc t) -> p cc t", cc=CC)
                                    [:, cc, sp * NSP:(sp + 1) * NSP],
                                    start=first, stop=False),
                                    deps=bank_dep(2 + sp) if cc == 0 else [],
                                    inc=False)
                            mgs = op("tensor", lambda e, m=m, sp=sp: e.matmul(
                                pb[0 + sp][:, 0:NSP],
                                bgb[:, m * 128:(m + 1) * 128],
                                ragf[0:72, sp * NSP:(sp + 1) * NSP],
                                start=False, stop=True),
                                deps=[cur[f"ra_{id(ragf)}_{t}"]
                                      for t in range(NTT)][-1:])
                            mus = op("tensor", lambda e, m=m, sp=sp: e.matmul(
                                pb[2 + sp][:, 0:NSP],
                                bub[:, m * 128:(m + 1) * 128],
                                rauf[0:72, sp * NSP:(sp + 1) * NSP],
                                start=False, stop=True),
                                deps=[cur[f"ra_{id(rauf)}_{t}"]
                                      for t in range(NTT)][-1:])
                            tsl = op("scalar", lambda e, sp=sp: e.activation(
                                sg[sp][:], pb[0 + sp][:, 0:NSP], ACT.Silu),
                                deps=[mgs] + cur.get(f"sg_free{sp}", []))
                            set_bank_dep(0 + sp, [tsl])
                            th = op("vector", lambda e, m=m, sp=sp:
                                    e.tensor_mul(
                                hb[:, m * BT + sp * NSP:
                                   m * BT + (sp + 1) * NSP],
                                sg[sp][:], pb[2 + sp][:, 0:NSP]),
                                deps=[tsl, mus])
                            set_bank_dep(2 + sp, [th])
                            cur[f"sg_free{sp}"] = [th]
                            hdone.append(th)
                        wgu_free[m % 2] = hdone
                        cur[f"h_{m}"] = hdone
                    run_phase(dma_waits=[
                        (DS("wgu0"), DS("wgu0").n), (DS("wgu1"), DS("wgu1").n),
                        (DS("fres"), DS("fres").n),
                        (DS(f"x1l0"), DS("x1l0").n),
                        (DS(f"x1l1"), DS("x1l1").n)])

                sF2 = ExitStack()
                with sF2:
                    wdb = [sb(sF2, f"wdb{i}_{blk}", [128, NF * 128], BF16)
                           for i in range(2)]
                    fc2 = sb(sF2, f"fc2_{blk}", [128, NSP], F32)
                    ftok = sb(sF2, f"ftok_{blk}", [128, (NTT // NSB) * C], F32)
                    x1l2 = [sb(sF2, f"x1l2{i}_{blk}", [128, C], F32)
                            for i in range(2)]
                    raf2 = sb(sF2, f"raf2_{blk}", [128, 72], F32)

                    bank_deps = {i: [] for i in range(8)}
                    phase(f"f2_{blk}")
                    hdep = [(sv["vector"], sv["vector"].n)]
                    for tt in range(NTT):
                        mm = None
                        for fc in range(NF):
                            mm = op("tensor", lambda e, fc=fc, tt=tt,
                                    first=(fc == 0), last=(fc == NF - 1):
                                    e.matmul(
                                pb[6][:, 0:72],
                                hb[:, fc * BT + tt * 128:
                                   fc * BT + (tt + 1) * 128],
                                adb[:, fc * 72:(fc + 1) * 72],
                                start=first, stop=last),
                                deps=(hdep + bank_dep(6)) if fc == 0 else [],
                                inc=(fc == NF - 1))
                        t1 = op("vector", lambda e, tt=tt: e.tensor_tensor(
                            raf2[:, 0:64].rearrange("p (x r) -> p x r", x=8),
                            pb[6][:, 0:64].rearrange("p (x r) -> p x r", x=8),
                            wsc[:, tt * 8:(tt + 1) * 8].unsqueeze(2)
                            .broadcast_to([128, 8, 8]), op=ALU.mult),
                            deps=[mm])
                        t2 = op("vector", lambda e: e.tensor_scalar_mul(
                            raf2[:, 64:72], pb[6][:, 64:72], 0.125), deps=[t1])
                        set_bank_dep(6, [t2])
                        ttr = op("tensor", lambda e: e.transpose(
                            pb[7][0:72, 0:128], raf2[:, 0:72], idf3[:]),
                            deps=[t1, t2,
                                  (DS("fres2"), DS("fres2").n)] + bank_dep(7))
                        tre = op("vector", lambda e, tt=tt: e.tensor_copy(
                            rad[0:72, tt * 128:(tt + 1) * 128],
                            pb[7][0:72, 0:128]), deps=[ttr])
                        set_bank_dep(7, [tre])
                        cur[f"rad_{tt}"] = tre
                    rad_done = [(sv["vector"], sv["vector"].n)]

                    wd_free = {0: [], 1: []}
                    x1_free3 = {0: [], 1: []}
                    out_free = {0: [], 1: []}
                    for half in range(NSB):
                        ft_done = []
                        for cb in range(CC):
                            wb_ = wdb[cb % 2]
                            lw = dma("gpsimd",
                                     wb_[:].rearrange(
                                         "p (fc z) -> p fc z", fc=NF),
                                     wd[cb].rearrange(
                                         "(fc p) z -> p fc z", p=128),
                                     f"wd{cb % 2}", deps=wd_free[cb % 2])
                            mm = None
                            for fc in range(NF):
                                mm = op("tensor", lambda e, fc=fc, cb=cb,
                                        half=half, wb_=wb_, first=(fc == 0):
                                        e.matmul(
                                    pb[0 + (cb % 2)][:, 0:NSP],
                                    wb_[:, fc * 128:(fc + 1) * 128],
                                    hb[:, fc * BT + half * NSP:
                                       fc * BT + (half + 1) * NSP],
                                    start=first, stop=False),
                                    deps=([lw] + hdep + bank_dep(0 + (cb % 2)))
                                    if fc == 0 else [], inc=False)
                            mm = op("tensor", lambda e, cb=cb, half=half:
                                    e.matmul(
                                pb[0 + (cb % 2)][:, 0:NSP],
                                bdb[:, cb * 128:(cb + 1) * 128],
                                rad[0:72, half * NSP:(half + 1) * NSP],
                                start=False, stop=True),
                                deps=rad_done)
                            tev = op("vector", lambda e, cb=cb: e.tensor_copy(
                                fc2[:], pb[0 + (cb % 2)][:, 0:NSP]),
                                deps=[mm] + cur.get("fc2_free", []))
                            set_bank_dep(0 + (cb % 2), [tev])
                            wd_free[cb % 2] = [tev]
                            nq = NSP // 128
                            tprev = []
                            for q in range(nq):
                                bk = pb[4 + (q % 2)]
                                ttr = op("tensor", lambda e, q=q, bk=bk:
                                         e.transpose(
                                    bk[:, 0:128],
                                    fc2[:, q * 128:(q + 1) * 128], idf3[:]),
                                    deps=[tev,
                                          (DS("fres2"), DS("fres2").n)] +
                                    bank_dep(4 + (q % 2)))
                                tfe = op("vector", lambda e, q=q, cb=cb,
                                         bk=bk: e.tensor_copy(
                                    ftok[:, q * C + cb * 128:
                                         q * C + (cb + 1) * 128],
                                    bk[:, 0:128]), deps=[ttr])
                                set_bank_dep(4 + (q % 2), [tfe])
                                tprev.append(tfe)
                            ft_done.extend(tprev)
                            cur["fc2_free"] = tprev
                        for q in range(NSP // 128):
                            tt = half * (NSP // 128) + q
                            trow = blk * BT + tt * 128
                            ld = dma("scalar", x1l2[q % 2][:],
                                     out[trow:trow + 128, :],
                                     f"x1f{q % 2}",
                                     deps=[(DS("x1st0"), DS("x1st0").n),
                                           (DS("x1st1"), DS("x1st1").n)] +
                                     x1_free3[q % 2])
                            tad = op("vector", lambda e, q=q: e.tensor_add(
                                x1l2[q % 2][:], ftok[:, q * C:(q + 1) * C],
                                x1l2[q % 2][:]),
                                deps=[ld] + ft_done)
                            std = dma("sync", out[trow:trow + 128, :],
                                      x1l2[q % 2][:], f"outst{q % 2}", deps=[tad])
                            x1_free3[q % 2] = [std]
                    run_phase(dma_waits=[
                        (DS("outst0"), DS("outst0").n),
                        (DS("outst1"), DS("outst1").n),
                        (DS("wd0"), DS("wd0").n), (DS("wd1"), DS("wd1").n),
                        (DS("x1f0"), DS("x1f0").n),
                        (DS("x1f1"), DS("x1f1").n)])

    return nc



def _prep_inputs(ins, cfg):
    C = cfg["C"]; T = cfg["T"]; TOWN = cfg["TOWN"]; H = cfg["H"]; G = cfg["G"]
    DFF = cfg["DFF"]
    CC = C // 128; NT = T // 128; NJ = TOWN // 128; KV = G * 128
    HP = H // 2; NF = DFF // 128; NV = NT + 1

    f32 = np.float32
    import ml_dtypes
    bf16 = ml_dtypes.bfloat16

    wqT = np.ascontiguousarray(ins["L0_attn_Wq"].T)
    wqp = np.ascontiguousarray(
        wqT.reshape(C, G, 512).transpose(1, 0, 2))
    wkT = np.ascontiguousarray(ins["L0_attn_Wk"].T)
    wvT = np.ascontiguousarray(ins["L0_attn_Wv"].T)
    woT = np.ascontiguousarray(ins["L0_attn_Wo"].T)

    Wg = ins["L0_ffn_W_gate"]; Wu = ins["L0_ffn_W_up"]; Wd = ins["L0_ffn_W_down"]
    wgu = np.ascontiguousarray(np.stack(
        [Wg.reshape(C, NF, 128).transpose(1, 0, 2),
         Wu.reshape(C, NF, 128).transpose(1, 0, 2)], axis=1))
    wdp = np.ascontiguousarray(
        Wd.reshape(DFF, CC, 128).transpose(1, 0, 2))

    agc = np.concatenate(
        [ins["L0_ffn_A_gate"].transpose(1, 0, 2).reshape(C, 64),
         ins["L0_ffn_sA_gate"]], axis=1)
    auc = np.concatenate(
        [ins["L0_ffn_A_up"].transpose(1, 0, 2).reshape(C, 64),
         ins["L0_ffn_sA_up"]], axis=1)
    aguc = np.ascontiguousarray(np.concatenate(
        [agc, auc, ins["L0_ffn_W_router"]], axis=1))
    adc = np.ascontiguousarray(np.concatenate(
        [ins["L0_ffn_A_down"].transpose(1, 0, 2).reshape(DFF, 64),
         ins["L0_ffn_sA_down"]], axis=1))
    bgc = np.ascontiguousarray(np.concatenate(
        [ins["L0_ffn_B_gate"].reshape(64, DFF), ins["L0_ffn_sB_gate"]], axis=0))
    buc = np.ascontiguousarray(np.concatenate(
        [ins["L0_ffn_B_up"].reshape(64, DFF), ins["L0_ffn_sB_up"]], axis=0))
    bdc = np.ascontiguousarray(np.concatenate(
        [ins["L0_ffn_B_down"].reshape(64, C), ins["L0_ffn_sB_down"]], axis=0))

    inv_freq = 1.0 / 10000.0 ** (np.arange(0, 128, 2, dtype=np.float64) / 128)
    tt = np.arange(T, dtype=np.float64)
    fr = np.outer(tt, inv_freq)
    cosg = np.cos(fr).astype(f32)
    sing = np.sin(fr).astype(f32)

    sm = (np.arange(T)[:, None] >= 128 * np.arange(NV)[None, :]).astype(bf16)

    ident = np.eye(128, dtype=f32)
    identb = np.eye(128).astype(bf16)

    x = ins["L0_attn_x"]
    B = x.shape[0]
    maps = []
    for core in range(B * 2):
        b, s = core // 2, core % 2
        xb = np.ascontiguousarray(x[b])
        own = np.ascontiguousarray(
            x[b].reshape(NT, 128, C)[s::2].reshape(TOWN, C))
        coso = np.ascontiguousarray(
            cosg.reshape(NT, 128, 64)[s::2].reshape(TOWN, 64))
        sino = np.ascontiguousarray(
            sing.reshape(NT, 128, 64)[s::2].reshape(TOWN, 64))
        kp = np.arange(128)[:, None]
        t_ = np.arange(256)[None, :] // 128
        qq = np.arange(256)[None, :] % 128
        cm = np.where(t_ * 128 + kp <= qq + 128 * s, 2.0, -1.0).astype(f32)
        maps.append(dict(
            xf=xb, xo=own, wq=wqp, wk=wkT, wv=wvT, wo=woT,
            qnw=ins["L0_attn_q_norm_w"], knw=ins["L0_attn_k_norm_w"],
            bo=ins["L0_attn_bo"],
            ebias=ins["L0_ffn_expert_bias"], wgu=wgu, wd=wdp,
            agu=aguc, ad=adc, bg=bgc, bu=buc, bd=bdc,
            cosg=cosg, sing=sing, coso=coso, sino=sino,
            cmaskin=cm, smask=sm,
            cvals=np.array([1e-5, -50.0], np.float32),
            idf_in=ident, idb_in=identb))
    return maps


def kernel(**inputs):
    cfg = FULL
    maps = _prep_inputs(inputs, cfg)
    nc = build(cfg)
    res = run_bass_kernel_spmd(nc, maps, core_ids=list(range(8)))
    T = cfg["T"]; C = cfg["C"]; NT = T // 128; NJ = cfg["TOWN"] // 128
    B = inputs["L0_attn_x"].shape[0]
    outp = np.empty((B, T, C), np.float32)
    for core in range(8):
        b, s = core // 2, core % 2
        outp[b].reshape(NT, 128, C)[s::2] = \
            res.results[core]["out"].reshape(NJ, 128, C)
    return outp
